# version 10
# baseline (speedup 1.0000x reference)
import numpy as np
import ml_dtypes
import concourse.bass as bass
import concourse.mybir as mybir
from concourse.bass_utils import run_bass_kernel_spmd

F32 = mybir.dt.float32
BF16 = mybir.dt.bfloat16
AF = mybir.ActivationFunctionType
ALU = mybir.AluOpType
AX = mybir.AxisListType.X

T, C, II, E, NH, HD = 2048, 1024, 4096, 8, 16, 64
NJ, NM, NJB = 16, 8, 4
LN_EPS = 1e-5
GATE_EPS = 1e-6
ATT_EPS = 1e-6
AW_CLAMP = 60.0

TAP = None
PHASE = 3
N_EXP = E
LN_ONLY = False
LN_LVL = 3
NJL = NJ


class Em:
    ENGS = ("sp", "pe", "dve", "act", "gp")

    def __init__(self):
        self.q = {e: [] for e in self.ENGS}
        self.cnt = {}
        self.waited = {e: {} for e in self.ENGS}
        self.hw = {}

    def wait(self, eng, tok):
        if tok is None:
            return
        if isinstance(tok, list):
            for t in tok:
                self.wait(eng, t)
            return
        sem, val = tok
        if self.waited[eng].get(sem, 0) >= val:
            return
        self.waited[eng][sem] = val
        self.q[eng].append(("w", sem, val))

    def scope(self, name):
        for e in self.ENGS:
            self.q[e].append(("s", name))

    def escope(self):
        for e in self.ENGS:
            self.q[e].append(("e",))

    def sop(self, eng, make):
        self.wait(eng, (eng, self.cnt.get(eng, 0)))
        return self.op(eng, make, inc=eng)

    def op(self, eng, make, inc=None, n=1):
        tok = None
        if inc is not None:
            self.cnt[inc] = self.cnt.get(inc, 0) + n
            tok = (inc, self.cnt[inc])
        self.q[eng].append(("i", make, inc, n))
        return tok

    def install(self, engine, eng_name, nc=None):
        hw = self.hw
        sstack = []
        for item in self.q[eng_name]:
            if item[0] == "w":
                engine.wait_ge(hw[item[1]], item[2])
            elif item[0] == "s":
                if nc is not None:
                    sid, _ = nc.enter_named_scope(item[1], False)
                    sstack.append((item[1], sid))
            elif item[0] == "e":
                if nc is not None and sstack:
                    nm, sid = sstack.pop()
                    nc.leave_named_scope(nm, sid, False)
            else:
                _, make, inc, n = item
                inst = make(engine)
                if inc is not None:
                    inst.then_inc(hw[inc], n)


def build_nc(tap=None):
    nc = bass.Bass("TRN2", target_bir_lowering=False,
                   detect_race_conditions=False)

    x_d = nc.dram_tensor("x", [T, C], F32, kind="ExternalInput").ap()
    wupT_d = nc.dram_tensor("wupT", [E, C, II], F32, kind="ExternalInput").ap()
    wdnT_d = nc.dram_tensor("wdnT", [E, II, C], F32, kind="ExternalInput").ap()
    wqkvT_d = nc.dram_tensor("wqkvT", [C, 3 * C], F32, kind="ExternalInput").ap()
    woutT_d = nc.dram_tensor("woutT", [C, C], F32, kind="ExternalInput").ap()
    gwT_d = nc.dram_tensor("gwT", [C, E], F32, kind="ExternalInput").ap()
    eb_d = nc.dram_tensor("eb", [1, E], F32, kind="ExternalInput").ap()
    hyp_d = nc.dram_tensor("hyp", [1, NH], F32, kind="ExternalInput").ap()
    kcmap_d = nc.dram_tensor("kcmap", [128, 4], F32, kind="ExternalInput").ap()
    crep_d = nc.dram_tensor("crep", [128, T], BF16, kind="ExternalInput").ap()
    srep_d = nc.dram_tensor("srep", [128, T], BF16, kind="ExternalInput").ap()
    maskT_d = nc.dram_tensor("maskT", [128, 4 * 512], BF16, kind="ExternalInput").ap()
    iden_d = nc.dram_tensor("iden", [128, 128], BF16, kind="ExternalInput").ap()
    blkd_d = nc.dram_tensor("blkd", [128, 128], BF16, kind="ExternalInput").ap()
    ones_d = nc.dram_tensor("onesm", [128, 128], F32, kind="ExternalInput").ap()

    y_d = nc.dram_tensor("y", [T, C], F32, kind="ExternalOutput").ap()
    xA_d = nc.dram_tensor("xA", [T, C], F32, kind="Internal").ap()
    xB_d = nc.dram_tensor("xB", [T, C], F32, kind="Internal").ap()
    scrup_d = nc.dram_tensor("scrup", [E, 2 * NJ, 128, 1024], BF16,
                             kind="Internal").ap()
    scrdn_d = nc.dram_tensor("scrdn", [E, 2, 2, 128, 8192], BF16,
                             kind="Internal").ap()

    wupT_r = [wupT_d[e].rearrange("(m p) i -> p m i", p=128) for e in range(E)]
    wdnT_r = [wdnT_d[e].rearrange("(n p) h -> p n h", p=128) for e in range(E)]
    wqkvT_r = wqkvT_d.rearrange("(m p) f -> p m f", p=128)
    wqkvT_r4 = wqkvT_d.rearrange("(m p) (h d) -> p m h d", p=128, d=HD)
    woutT_r = woutT_d.rearrange("(m p) c -> p m c", p=128)
    gwT_r = gwT_d.rearrange("(m p) e -> p m e", p=128)

    SB_BASE = 16896

    def SBA(name, shape, dtype, offset):
        return nc.alloc_sbuf_tensor_at(name, shape, dtype, offset=offset + SB_BASE)
    lnxT = SBA("lnxT", [128, NM * T], BF16, offset=0)
    B0 = 32768
    OM = SBA("OM", [128, NJ * C], BF16, offset=B0)
    EBUF = [SBA(f"ebuf{i}", [128, 16 * 512], BF16, offset=B0 + 16384 * i)
            for i in range(2)]
    WVBF = SBA("wvbf", [128, NM * 1024], BF16, offset=B0)
    WOUTB = SBA("woutb", [128, NM * 1024], BF16, offset=B0)
    C0 = 65536
    HT = SBA("HT", [128, 16 * T], BF16, offset=C0)
    VSB = SBA("vsb", [128, NJ * 1056], BF16, offset=C0)
    R1Q = SBA("r1q", [128, T], BF16, offset=C0 + 33792)
    R2Q = SBA("r2q", [128, T], BF16, offset=C0 + 33792 + 4096)
    R1K = SBA("r1k", [128, T], BF16, offset=C0 + 33792 + 8192)
    R2K = SBA("r2k", [128, T], BF16, offset=C0 + 33792 + 12288)
    NK0 = SBA("nk0", [128, T], BF16, offset=C0 + 50176)
    Q0B = SBA("q0b", [128, T], BF16, offset=C0 + 54272)
    MASKT = SBA("maskt", [128, 4 * 512], BF16, offset=C0 + 58368)
    D0 = 131072
    WDN = [SBA(f"wdn{i}", [128, 16 * 512], BF16, offset=D0 + 16384 * i)
           for i in range(3)]
    OT = SBA("oT", [128, NM * T], BF16, offset=D0)
    CHZ = [SBA(f"chz{i}", [128, 512], F32, offset=D0 + 32768 + 2048 * i)
           for i in range(2)]
    ZP = [SBA(f"zp{i}", [128, 1024], F32, offset=D0 + 32768 + 4096 * i)
          for i in range(2)]
    ZP1A = ZP[1].ap()[:, :]
    CHT = [SBA(f"cht{i}", [128, 512], F32, offset=D0 + 36864 + 2048 * i)
           for i in range(2)]
    LNBF = SBA("lnbf", [128, C], BF16, offset=D0 + 32768)
    LNBF2 = SBA("lnbf2", [128, C], BF16, offset=D0 + 34816)
    SQS = SBA("sqs", [128, 512], BF16, offset=D0 + 40960)
    SQS2 = SBA("sqs2", [128, 512], BF16, offset=D0 + 41984)
    REPR = SBA("repr", [128, 512], F32, offset=D0 + 40960)
    SSB = SBA("ssb", [128, 512], F32, offset=D0 + 43008)
    WQK = [SBA(f"wqk{i}", [128, 1024], BF16, offset=D0 + 45056 + 2048 * i)
           for i in range(2)]
    WUPS = [SBA("wups0", [128, 1024], BF16, offset=180224 + 16384),
            SBA("wups1", [128, 1024], BF16, offset=198656 + 11104)]
    E0 = 180224
    POOL = [SBA(f"pool{i}", [128, 1024], F32, offset=E0 + 4096 * i) for i in range(4)]
    F0 = 198656
    CREPS = SBA("creps", [128, T], BF16, offset=F0)
    SREPS = SBA("sreps", [128, T], BF16, offset=F0 + 4096)
    IDEN = SBA("iden_s", [128, 128], BF16, offset=F0 + 8192)
    BLKD = SBA("blkd_s", [128, 128], BF16, offset=F0 + 8448)
    ONES = SBA("ones_s", [128, 128], F32, offset=F0 + 8704)
    CONSTS = SBA("consts", [128, 64], F32, offset=F0 + 9216)
    KCMAP = SBA("kcmap_s", [128, 4], F32, offset=F0 + 9472)
    HROW = SBA("hrow", [128, 64], F32, offset=F0 + 9504)
    EBR = SBA("ebr", [128, 8], F32, offset=F0 + 9760)
    GWB = SBA("gwb", [128, 64], BF16, offset=F0 + 9792)
    SPARSE = SBA("sparse", [128, NJ * E], F32, offset=F0 + 9920)
    STNM1 = SBA("stnm1", [128, NJ], F32, offset=F0 + 10432)
    STR1 = SBA("str1", [128, NJ], F32, offset=F0 + 10496)
    STNM3 = SBA("stnm3", [128, NJ], F32, offset=F0 + 10560)
    STR3 = SBA("str3", [128, NJ], F32, offset=F0 + 10624)
    SC = [SBA(f"sc{i}", [128, 1], F32, offset=F0 + 10688 + 32 * i) for i in range(8)]
    SC2 = [SBA(f"sc2{i}", [128, 1], F32, offset=F0 + 13152 + 32 * i)
           for i in range(4)]
    GSC = SBA("gsc", [128, 32], F32, offset=F0 + 10944)
    CBIAS = SBA("cbias", [128, 4], F32, offset=F0 + 11072)

    PBP = [nc.alloc_psum_tensor(f"pbp{i}", [128, 1024], F32) for i in range(3)]
    PBS6 = nc.alloc_psum_tensor("pbs6", [128, 512], F32)
    PB = [PBP[i // 2][:, (i % 2) * 512:(i % 2) * 512 + 512] for i in range(6)]
    PB = PB + [PBS6[:, :]]
    PB5AP = PBP[2].ap()[:, 512:1024]
    PT = nc.alloc_psum_tensor("pt", [128, 1024], BF16)

    em = Em()
    pe, dve, act = "pe", "dve", "act"

    NLD, NST = 10, 6
    LD_SEMS = [f"ld{i}" for i in range(NLD)]
    ST_SEMS = [f"st{i}" for i in range(NST)]
    _ld_i = [0]
    _st_i = [0]

    def dma_in(dst, src):
        s = LD_SEMS[_ld_i[0] % NLD]
        _ld_i[0] += 1
        em.wait("sp", (s, em.cnt.get(s, 0)))
        return em.op("sp", lambda e, d=dst, s2=src: e.dma_start(out=d, in_=s2),
                     inc=s, n=16)

    def dma_out(dst, src):
        s = ST_SEMS[_st_i[0] % NST]
        _st_i[0] += 1
        em.wait("act", (s, em.cnt.get(s, 0)))
        return em.op("act", lambda e, d=dst, s2=src: e.dma_start(out=d, in_=s2),
                     inc=s, n=16)

    pool_free = [None, None, None, None]
    pt_free = [None]
    lnbf_free = [None, None]
    lnxT_tok = [None]
    pb_free = [None] * 7
    up_slot_free = [None, None]
    wdn_free = [None, None, None]
    wdn_seq = [0]

    em.op("gp", lambda e: e.memset(CBIAS[:, 0:1], LN_EPS))
    em.op("gp", lambda e: e.memset(CBIAS[:, 1:2], -1.0))
    tok_cbias = em.op("gp", lambda e: e.memset(CBIAS[:, 2:3], ATT_EPS), inc="gp")
    em.wait("act", tok_cbias)
    BIAS_LNEPS = CBIAS[:, 0:1]
    BIAS_NEG1 = CBIAS[:, 1:2]
    BIAS_ATTEPS = CBIAS[:, 2:3]

    tc = [dma_in(CREPS[:, :], crep_d), dma_in(SREPS[:, :], srep_d),
          dma_in(IDEN[:, :], iden_d), dma_in(BLKD[:, :], blkd_d),
          dma_in(ONES[:, :], ones_d), dma_in(KCMAP[:, :], kcmap_d),
          dma_in(HROW[0:1, 0:NH], hyp_d), dma_in(EBR[0:1, :], eb_d)]
    t_gw = dma_in(POOL[3][:, 0:64], gwT_r[:, :, :])
    em.wait("act", tc)
    em.sop("act", lambda e: e.activation(out=HROW[0:1, 48:64], in_=HROW[0:1, 0:16],
                                         func=AF.Ln))
    em.sop("act", lambda e: e.activation(out=HROW[0:1, 32:48], in_=HROW[0:1, 48:64],
                                         func=AF.Exp, scale=0.5))
    t1 = em.sop("act", lambda e: e.activation(out=HROW[0:1, 16:32],
                                              in_=HROW[0:1, 48:64],
                                              func=AF.Exp, scale=-1.0))
    em.wait("dve", t1)
    t2 = em.op("dve", lambda e: e.tensor_scalar_mul(
        out=HROW[0:1, 16:32], in0=HROW[0:1, 16:32], scalar1=-1.0), inc=dve)
    em.wait("pe", [t1, t2] + tc)
    em.op("pe", lambda e: e.matmul(PBP[0][:, 0:48], ONES[0:1, 0:128],
                                   HROW[0:1, 0:48], start=True, stop=True))
    t4 = em.op("pe", lambda e: e.matmul(PBP[0][:, 64:72], ONES[0:1, 0:128],
                                        EBR[0:1, :], start=True, stop=True), inc=pe)
    em.wait("dve", [t4, t_gw])
    em.op("dve", lambda e: e.tensor_copy(out=CONSTS[:, 0:48], in_=PBP[0][:, 0:48]))
    em.op("dve", lambda e: e.tensor_copy(out=EBR[:, :], in_=PBP[0][:, 64:72]))
    tok_consts = em.op("dve", lambda e: e.tensor_copy(out=GWB[:, :],
                                                      in_=POOL[3][:, 0:64]), inc=dve)
    pb_free[0] = tok_consts
    pool_free[3] = tok_consts

    def emit_ln_rows(src_ap, xsq_ap, out_ap, wait_toks, nm_store=None, r_store=None,
                     alt=False):
        sset = SC2 if alt else SC
        nm, r, v1, v2 = sset[0][:, :], sset[1][:, :], sset[2][:, :], sset[3][:, :]
        em.wait("dve", wait_toks)
        if LN_LVL == 1:
            return em.sop("act", lambda e: e.activation(out=out_ap, in_=src_ap,
                                                        func=AF.Identity))
        em.sop("dve", lambda e: e.reduce_sum(out=nm, in_=src_ap, axis=AX, negate=True))
        em.sop("dve", lambda e: e.tensor_mul(out=xsq_ap, in0=src_ap, in1=src_ap))
        em.sop("dve", lambda e: e.reduce_sum(out=v1, in_=xsq_ap, axis=AX))
        em.sop("dve", lambda e: e.tensor_scalar_mul(out=nm, in0=nm, scalar1=1.0 / C))
        em.sop("dve", lambda e: e.tensor_scalar_mul(out=v1, in0=v1, scalar1=1.0 / C))
        em.sop("dve", lambda e: e.tensor_mul(out=v2, in0=nm, in1=nm))
        tv = em.sop("dve", lambda e: e.tensor_sub(out=v1, in0=v1, in1=v2))
        if LN_LVL == 2:
            em.wait("act", tv)
            return em.sop("act", lambda e: e.activation(out=out_ap, in_=src_ap,
                                                        func=AF.Identity))
        em.wait("act", tv)
        em.sop("act", lambda e: e.activation(out=v2, in_=v1, func=AF.Ln,
                                             bias=BIAS_LNEPS))
        tr = em.sop("act", lambda e: e.activation(out=r, in_=v2, func=AF.Exp,
                                                  scale=-0.5))
        em.wait("dve", tr)
        tb = em.sop("dve", lambda e: e.tensor_mul(out=v2, in0=nm, in1=r))
        if nm_store is not None:
            em.sop("dve", lambda e, d=nm_store: e.tensor_copy(out=d, in_=nm))
            em.sop("dve", lambda e, d=r_store: e.tensor_copy(out=d, in_=r))
        em.wait("act", tb)
        ta = em.sop("act", lambda e: e.activation(out=out_ap, in_=src_ap,
                                                  func=AF.Identity,
                                                  bias=v2, scale=r))
        return ta

    def emit_transpose_chunk(j, wait_tok, src=None):
        buf = LNBF if src is None else src
        em.wait("pe", [wait_tok, pt_free[0]])
        tt = None
        for m in range(NM):
            tt = em.op("pe", lambda e, m=m, buf=buf: e.transpose(
                PT[:, m * 128:(m + 1) * 128], buf[:, m * 128:(m + 1) * 128],
                IDEN[:, :]), inc=pe if m == NM - 1 else None)
        em.wait("dve", tt)
        te = None
        for m in range(NM):
            te = em.op("dve", lambda e, m=m, j=j: e.tensor_copy(
                out=lnxT[:, m * T + j * 128: m * T + j * 128 + 128],
                in_=PT[:, m * 128:(m + 1) * 128]),
                inc=dve if m == NM - 1 else None)
        pt_free[0] = te
        lnbf_free[0 if buf is LNBF else 1] = te
        return te

    def emit_ln_pass(src_dram, nm_store_t, r_store_t):
        last = None
        for j in range(NJL):
            slot = j % 2
            em.wait("sp", pool_free[slot])
            tl = dma_in(POOL[slot][:, :], src_dram[j * 128:(j + 1) * 128, :])
            xs = 2 + (j % 2)
            lb = LNBF if j % 2 == 0 else LNBF2
            em.wait("dve", pool_free[xs])
            em.wait("act", lnbf_free[j % 2])
            ta = emit_ln_rows(
                POOL[slot][:, :], POOL[xs][:, :], lb[:, :], [tl],
                nm_store=None if nm_store_t is None else nm_store_t[:, j:j + 1],
                r_store=None if r_store_t is None else r_store_t[:, j:j + 1],
                alt=(j % 2 == 1))
            pool_free[slot] = ta
            pool_free[xs] = ta
            last = emit_transpose_chunk(j, ta, src=lb)
        lnxT_tok[0] = last
        return last

    def emit_gate():
        em.wait("pe", [lnxT_tok[0], tok_consts, pb_free[6]])
        pb6 = [None]
        last = None
        for j in range(NJ):
            em.wait("pe", pb6[0])
            tg = None
            for m in range(NM):
                tg = em.op("pe", lambda e, m=m, j=j: e.matmul(
                    PBS6[:, 0:8],
                    lnxT[:, m * T + j * 128: m * T + j * 128 + 128],
                    GWB[:, m * 8:(m + 1) * 8],
                    start=(m == 0), stop=(m == NM - 1)),
                    inc=pe if m == NM - 1 else None)
            lg, ex, q1m, q2m = GSC[:, 0:8], GSC[:, 8:16], GSC[:, 16:24], GSC[:, 24:32]
            mx, sall, m2, den = SC[4][:, :], SC[5][:, :], SC[6][:, :], SC[7][:, :]
            em.wait("dve", tg)
            tpb = em.sop("dve", lambda e: e.tensor_add(out=lg, in0=PBS6[:, 0:8],
                                                       in1=EBR[:, :]))
            pb6[0] = tpb
            tmx = em.sop("dve", lambda e: e.reduce_max(out=mx, in_=lg, axis=AX,
                                                       negate=True))
            em.wait("act", tmx)
            tex = em.sop("act", lambda e: e.activation(out=ex, in_=lg, func=AF.Exp,
                                                       bias=mx, accum_out=sall))
            em.wait("dve", tex)
            em.sop("dve", lambda e: e.reduce_max(out=m2, in_=ex, axis=AX))
            em.sop("dve", lambda e: e.tensor_scalar(out=q1m, in0=ex, scalar1=m2,
                                                    scalar2=None, op0=ALU.is_ge))
            em.sop("dve", lambda e: e.scalar_tensor_tensor(
                out=q1m, in0=q1m, scalar=-1e30, in1=ex, op0=ALU.mult, op1=ALU.add))
            em.sop("dve", lambda e: e.reduce_max(out=m2, in_=q1m, axis=AX))
            em.sop("dve", lambda e: e.tensor_scalar(out=q2m, in0=ex, scalar1=m2,
                                                    scalar2=None, op0=ALU.is_ge))
            em.sop("dve", lambda e: e.tensor_mul(out=q2m, in0=q2m, in1=ex))
            em.sop("dve", lambda e: e.reduce_sum(out=m2, in_=q2m, axis=AX))
            tden = em.sop("dve", lambda e: e.scalar_tensor_tensor(
                out=den, in0=sall, scalar=GATE_EPS, in1=m2, op0=ALU.mult,
                op1=ALU.add))
            em.wait("act", tden)
            em.sop("act", lambda e: e.activation(out=den, in_=den, func=AF.Ln))
            trd = em.sop("act", lambda e: e.activation(out=den, in_=den, func=AF.Exp,
                                                       scale=-1.0))
            em.wait("dve", trd)
            last = em.sop("dve", lambda e, j=j: e.tensor_scalar_mul(
                out=SPARSE[:, j * 8:(j + 1) * 8], in0=q2m, scalar1=den))
        pb_free[6] = tpb
        return last

    def emit_ffn(first_pass, sparse_tok, scr_wait=None):
        gelu_last = [None]
        om_started = [False]
        scr_last = [None]
        for e_i in range(N_EXP):
            for half in range(2):
                n0 = half * 16
                for k in range(16):
                    n = n0 + k
                    bs = n % 2
                    if first_pass:
                        ps = n % 2
                        em.wait("sp", pool_free[ps])
                        tl = dma_in(POOL[ps][:, :],
                                    wupT_r[e_i][:, :, n * 128:(n + 1) * 128])
                        em.wait("dve", [tl, up_slot_free[bs]])
                        tb = em.op("dve", lambda e, ps=ps, bs=bs: e.tensor_copy(
                            out=WUPS[bs][:, :], in_=POOL[ps][:, :]), inc=dve)
                        pool_free[ps] = tb
                        em.wait("act", tb)
                        tw = dma_out(scrup_d[e_i, n], WUPS[bs][:, :])
                        scr_last[0] = tw
                        new_free = [tw]
                    else:
                        em.wait("sp", [up_slot_free[bs], scr_wait])
                        tb = dma_in(WUPS[bs][:, :], scrup_d[e_i, n])
                        new_free = []
                    last_mm = None
                    tg = None
                    for Jb in range(NJB):
                        bank = (n * NJB + Jb) % 6
                        em.wait("pe", [tb, pb_free[bank], lnxT_tok[0]])
                        for m in range(NM):
                            last_mm = em.op(
                                "pe", lambda e, bank=bank, bs=bs, m=m, Jb=Jb: e.matmul(
                                    PB[bank],
                                    WUPS[bs][:, m * 128:(m + 1) * 128],
                                    lnxT[:, m * T + Jb * 512: m * T + (Jb + 1) * 512],
                                    start=(m == 0), stop=(m == NM - 1)),
                                inc=pe if m == NM - 1 else None)
                        em.wait("act", last_mm)
                        tg = em.op("act", lambda e, bank=bank, k=k, Jb=Jb: e.activation(
                            out=HT[:, k * T + Jb * 512: k * T + (Jb + 1) * 512],
                            in_=PB[bank], func=AF.Gelu), inc=act)
                        pb_free[bank] = tg
                    up_slot_free[bs] = new_free + [last_mm]
                    gelu_last[0] = tg
                for G in range(2):
                    ws = wdn_seq[0] % 3
                    wdn_seq[0] += 1
                    fill = []
                    if first_pass:
                        for s in range(8):
                            ps2 = 2 + (s % 2)
                            em.wait("sp", pool_free[ps2])
                            tl = dma_in(
                                POOL[ps2][:, :],
                                wdnT_r[e_i][:, n0 + 2 * s:n0 + 2 * s + 2,
                                            G * 512:(G + 1) * 512])
                            em.wait("dve", [tl, wdn_free[ws]])
                            td = em.op("dve", lambda e, ps2=ps2, ws=ws, s=s:
                                       e.tensor_copy(
                                           out=WDN[ws][:, s * 1024:(s + 1) * 1024],
                                           in_=POOL[ps2][:, :]), inc=dve)
                            pool_free[ps2] = td
                            fill.append(td)
                        em.wait("act", fill[-1])
                        tw2 = dma_out(scrdn_d[e_i, half, G], WDN[ws][:, :])
                        scr_last[0] = tw2
                        new_free = [tw2]
                    else:
                        em.wait("sp", [wdn_free[ws], scr_wait])
                        tfull = dma_in(WDN[ws][:, :], scrdn_d[e_i, half, G])
                        fill = [tfull] * 8
                        new_free = []
                    last_down = None
                    for j in range(NJ):
                        bank = ((half * 2 + G) * NJ + j) % 6
                        em.wait("pe", [pb_free[bank], gelu_last[0], sparse_tok])
                        for n_loc in range(16):
                            em.wait("pe", fill[n_loc // 2])
                            last_down = em.op(
                                "pe",
                                lambda e, bank=bank, n_loc=n_loc, j=j, ws=ws: e.matmul(
                                    PB[bank],
                                    HT[:, n_loc * T + j * 128:
                                       n_loc * T + j * 128 + 128],
                                    WDN[ws][:, n_loc * 512:(n_loc + 1) * 512],
                                    start=(n_loc == 0), stop=(n_loc == 15)),
                                inc=pe if n_loc == 15 else None)
                        em.wait("dve", [last_down, sparse_tok])
                        sp_col = SPARSE[:, j * 8 + e_i: j * 8 + e_i + 1]
                        dst = OM[:, j * C + G * 512: j * C + (G + 1) * 512]
                        if e_i == 0 and half == 0:
                            tv = em.op("dve", lambda e, bank=bank, d=dst, s=sp_col:
                                       e.tensor_scalar_mul(
                                           out=d, in0=PB[bank], scalar1=s),
                                       inc=dve)
                        else:
                            tv = em.op("dve", lambda e, bank=bank, d=dst, s=sp_col:
                                       e.scalar_tensor_tensor(
                                           out=d, in0=PB[bank], scalar=s, in1=d,
                                           op0=ALU.mult, op1=ALU.add), inc=dve)
                        pb_free[bank] = tv
                    wdn_free[ws] = new_free + [last_down]
        return scr_last[0]

    def emit_combine(x_src, nm_t, r_t, out_dram, also_y, ln_next, nm_n, r_n,
                     x_wait=None):
        store_toks = []
        lnom_free = [None]
        last = None
        for j in range(NJ):
            xp = j % 2
            xs = 2 + (j % 2)
            em.wait("sp", [pool_free[xp], x_wait])
            tl = dma_in(POOL[xp][:, :], x_src[j * 128:(j + 1) * 128, :])
            om_j = OM[:, j * C:(j + 1) * C]
            em.wait("dve", pool_free[xs])
            em.wait("act", lnom_free[0])
            tln = emit_ln_rows(om_j, POOL[xs][:, :], ZP1A, [],
                               alt=(j % 2 == 1))
            a_c, b_c = (SC[4][:, :], SC[5][:, :]) if j % 2 == 0 else (
                SC[6][:, :], SC[7][:, :])
            em.wait("dve", tln)
            em.sop("dve", lambda e, j=j: e.tensor_scalar(
                out=a_c, in0=r_t[:, j:j + 1], scalar1=0.5, scalar2=1.0,
                op0=ALU.mult, op1=ALU.add))
            em.sop("dve", lambda e, j=j: e.tensor_mul(
                out=b_c, in0=nm_t[:, j:j + 1], in1=r_t[:, j:j + 1]))
            tab = em.sop("dve", lambda e: e.tensor_scalar_mul(
                out=b_c, in0=b_c, scalar1=0.5))
            em.wait("act", [tl, tab])
            tx = em.op("act", lambda e, xp=xp: e.activation(
                out=POOL[xp][:, :], in_=POOL[xp][:, :], func=AF.Identity,
                scale=a_c, bias=b_c), inc=act)
            em.wait("dve", tx)
            txa = em.op("dve", lambda e, xp=xp: e.scalar_tensor_tensor(
                out=POOL[xp][:, :], in0=ZP1A, scalar=0.5, in1=POOL[xp][:, :],
                op0=ALU.mult, op1=ALU.add), inc=dve)
            lnom_free[0] = txa
            em.wait("act", txa)
            tso = dma_out(out_dram[j * 128:(j + 1) * 128, :], POOL[xp][:, :])
            store_toks.append(tso)
            toks = [txa, tso]
            if also_y:
                toks.append(dma_out(y_d[j * 128:(j + 1) * 128, :], POOL[xp][:, :]))
            if ln_next:
                lb = LNBF if j % 2 == 0 else LNBF2
                em.wait("act", lnbf_free[j % 2])
                ta = emit_ln_rows(
                    POOL[xp][:, :], POOL[xs][:, :], lb[:, :], [txa],
                    nm_store=None if nm_n is None else nm_n[:, j:j + 1],
                    r_store=None if r_n is None else r_n[:, j:j + 1],
                    alt=(j % 2 == 0))
                pool_free[xs] = ta
                last = emit_transpose_chunk(j, ta, src=lb)
                toks.append(ta)
            else:
                pool_free[xs] = tln
            pool_free[xp] = toks
        if ln_next:
            lnxT_tok[0] = last
        return store_toks

    def emit_attention():
        tmask = dma_in(MASKT[:, :], maskT_d)
        wv_toks = []
        for m in range(NM):
            for G in range(2):
                ps = (m * 2 + G) % 2
                em.wait("sp", pool_free[ps])
                tl = dma_in(POOL[ps][:, 0:512],
                            wqkvT_r[:, m, 2 * C + G * 512: 2 * C + (G + 1) * 512])
                em.wait("dve", [tl, lnxT_tok[0]])
                td = em.op("dve", lambda e, ps=ps, m=m, G=G: e.tensor_copy(
                    out=WVBF[:, m * 1024 + G * 512: m * 1024 + (G + 1) * 512],
                    in_=POOL[ps][:, 0:512]), inc=dve)
                pool_free[ps] = td
                wv_toks.append(td)
        em.wait("gp", lnxT_tok[0])
        tones = em.op("gp", lambda e: e.memset(VSB[:, :], 1.0), inc="gp")
        vsb_toks = []
        for j in range(NJ):
            for G in range(2):
                bank = (j * 2 + G) % 6
                em.wait("pe", [pb_free[bank], lnxT_tok[0]] + wv_toks)
                tmm = None
                for m in range(NM):
                    tmm = em.op("pe", lambda e, bank=bank, m=m, j=j, G=G: e.matmul(
                        PB[bank],
                        lnxT[:, m * T + j * 128: m * T + j * 128 + 128],
                        WVBF[:, m * 1024 + G * 512: m * 1024 + (G + 1) * 512],
                        start=(m == 0), stop=(m == NM - 1)),
                        inc=pe if m == NM - 1 else None)
                em.wait("dve", [tmm, tones])
                dst = bass.AP(VSB, j * 1056 + G * 8 * 66,
                              [[NJ * 1056, 128], [66, 8], [1, 64]])
                tv = em.op("dve", lambda e, bank=bank, d=dst: e.tensor_copy(
                    out=d, in_=PB[bank]), inc=dve)
                pb_free[bank] = tv
                vsb_toks.append(tv)
        ebuf_free = [None, None]
        chain_free = [None, None]
        pb5 = [pb_free[5]]
        pb6 = [pb_free[6]]
        last_attnv = [None]
        ot_tok = [None]
        last_tsc = [None]
        qk_free = [[lnxT_tok[0]], [lnxT_tok[0]]]
        pend = [None]
        pipe_prev = [None]
        pipe_pp = [0]

        def flush_attnv(item):
            if pipe_prev[0] is not None and pipe_prev[0]["epair"] is item[4]:
                chain_h2(pipe_prev[0])
                pipe_prev[0] = None
            h2, Jq2, slot2, njk2, e_pair2 = item
            voff2 = 64 * (h2 % 2)
            tav = None
            for jk in range(njk2):
                em.wait("pe", [e_pair2[jk // 2], pb5[0], pb6[0]] + vsb_toks)
                vsl = VSB[:, jk * 1056 + h2 * 66: jk * 1056 + h2 * 66 + 64]
                osl = VSB[:, jk * 1056 + h2 * 66 + 64: jk * 1056 + h2 * 66 + 65]
                esl = EBUF[slot2][:, jk * 512:(jk + 1) * 512]
                em.op("pe", lambda e, vsl=vsl, esl=esl, voff=voff2, jk=jk,
                      njk=njk2: e.matmul(PB5AP[voff:voff + 64, :], vsl, esl,
                                         start=(jk == 0), stop=(jk == njk - 1),
                                         tile_position=(0, voff)))
                tav = em.op("pe", lambda e, osl=osl, esl=esl, jk=jk, njk=njk2:
                            e.matmul(PBS6[64:65, :], osl, esl,
                                     start=(jk == 0), stop=(jk == njk - 1),
                                     tile_position=(0, 64)),
                            inc=pe if jk == njk2 - 1 else None)
            ebuf_free[slot2] = tav
            last_attnv[0] = tav
            em.wait("dve", tav)
            tss = em.op("dve", lambda e: e.tensor_copy(
                out=SSB[64:65, :], in_=PBS6[64:65, :]), inc=dve)
            em.wait("pe", [tss])
            trep = em.op("pe", lambda e, voff=voff2: e.matmul(
                PBS6[voff:voff + 64, :], ONES[64:65, 0:64],
                SSB[64:65, :], start=True, stop=True,
                tile_position=(64, voff)), inc=pe)
            em.wait("act", trep)
            tl6 = em.sop("act", lambda e, voff=voff2: e.activation(
                out=REPR[voff:voff + 64, :], in_=PBS6[voff:voff + 64, :],
                func=AF.Ln))
            trc2 = em.sop("act", lambda e, voff=voff2: e.activation(
                out=REPR[voff:voff + 64, :], in_=REPR[voff:voff + 64, :],
                func=AF.Exp, scale=-1.0))
            pb6[0] = tl6
            odst = OT[voff2:voff2 + 64,
                      (h2 // 2) * T + Jq2 * 512:(h2 // 2) * T + (Jq2 + 1) * 512]
            em.wait("dve", trc2)
            tot = em.op("dve", lambda e, odst=odst, voff=voff2: e.tensor_mul(
                out=odst, in0=PB5AP[voff:voff + 64, :],
                in1=REPR[voff:voff + 64, :]), inc=dve)
            pb5[0] = tot
            ot_tok[0] = tot

        for g in range(4):
            if pipe_prev[0] is not None:
                chain_h2(pipe_prev[0])
                pipe_prev[0] = None
            rope_toks = []
            for side in range(2):
                blocks = []
                for hf in range(2):
                    bs = hf
                    ps = 2 + hf
                    em.wait("sp", [pool_free[ps], qk_free[bs]])
                    hbase = side * NH + 4 * g
                    tl = []
                    for hh in range(4):
                        src = wqkvT_r4[:, :, hbase + hh, hf * 32:(hf + 1) * 32]
                        dsth = bass.AP(POOL[ps], hh * 32,
                                       [[1024, 128], [128, 8], [1, 32]])
                        tl.append(dma_in(dsth, src))
                    em.wait("dve", tl)
                    tb = em.op("dve", lambda e, ps=ps, bs=bs: e.tensor_copy(
                        out=WQK[bs][:, :], in_=POOL[ps][:, :]), inc=dve)
                    pool_free[ps] = tb
                    blocks.append(tb)
                R1 = R1Q if side == 0 else R1K
                R2 = R2Q if side == 0 else R2K
                t12 = None
                for J in range(NJB):
                    b1, b2 = (J * 2) % 4, (J * 2 + 1) % 4
                    em.wait("pe", [blocks[0], blocks[1], pb_free[b1], pb_free[b2],
                                   lnxT_tok[0]])
                    for m in range(NM):
                        em.op("pe", lambda e, b1=b1, m=m, J=J: e.matmul(
                            PB[b1][:, :], WQK[0][:, m * 128:(m + 1) * 128],
                            lnxT[:, m * T + J * 512: m * T + (J + 1) * 512],
                            start=(m == 0), stop=(m == NM - 1)))
                    for m in range(NM):
                        t12 = em.op("pe", lambda e, b2=b2, m=m, J=J: e.matmul(
                            PB[b2][:, :], WQK[1][:, m * 128:(m + 1) * 128],
                            lnxT[:, m * T + J * 512: m * T + (J + 1) * 512],
                            start=(m == 0), stop=(m == NM - 1)),
                            inc=pe if m == NM - 1 else None)
                    cs = CREPS[:, J * 512:(J + 1) * 512]
                    sn = SREPS[:, J * 512:(J + 1) * 512]
                    r1sl = R1[:, J * 512:(J + 1) * 512]
                    r2sl = R2[:, J * 512:(J + 1) * 512]
                    em.wait("dve", [t12, chain_free[0], chain_free[1],
                                    last_tsc[0]])
                    em.sop("dve", lambda e, b1=b1, cs=cs: e.tensor_mul(
                        out=CHZ[0][:, :], in0=PB[b1][:, :], in1=cs))
                    em.sop("dve", lambda e, b2=b2, sn=sn: e.tensor_mul(
                        out=CHT[0][:, :], in0=PB[b2][:, :], in1=sn))
                    em.sop("dve", lambda e, r1sl=r1sl: e.tensor_add(
                        out=r1sl, in0=CHZ[0][:, :], in1=CHT[0][:, :]))
                    em.sop("dve", lambda e, b2=b2, cs=cs: e.tensor_mul(
                        out=CHZ[0][:, :], in0=PB[b2][:, :], in1=cs))
                    em.sop("dve", lambda e, b1=b1, sn=sn: e.tensor_mul(
                        out=CHT[0][:, :], in0=PB[b1][:, :], in1=sn))
                    trr = em.sop("dve", lambda e, r2sl=r2sl: e.tensor_sub(
                        out=r2sl, in0=CHZ[0][:, :], in1=CHT[0][:, :]))
                    pb_free[b1] = trr
                    pb_free[b2] = trr
                    chain_free[0] = trr
                    chain_free[1] = trr
                    em.sop("dve", lambda e, r1sl=r1sl: e.tensor_mul(
                        out=SQS[:, :], in0=r1sl, in1=r1sl))
                    em.sop("dve", lambda e, r2sl=r2sl: e.tensor_mul(
                        out=SQS2[:, :], in0=r2sl, in1=r2sl))
                    tsq = em.sop("dve", lambda e: e.tensor_add(
                        out=SQS[:, :], in0=SQS[:, :], in1=SQS2[:, :]))
                    em.wait("pe", [tsq, pb6[0]])
                    tq0 = em.op("pe", lambda e: e.matmul(
                        PBS6[:, :], BLKD[:, :], SQS[:, :], start=True, stop=True),
                        inc=pe)
                    em.wait("act", [tq0, chain_free[1], last_tsc[0]])
                    kcc = KCMAP[:, g:g + 1]
                    em.sop("act", lambda e, kcc=kcc: e.activation(
                        out=CHT[1][:, :], in_=PBS6[:, :], func=AF.Ln, bias=kcc))
                    if side == 0:
                        tq = em.sop("act", lambda e, J=J: e.activation(
                            out=Q0B[:, J * 512:(J + 1) * 512], in_=CHT[1][:, :],
                            func=AF.Exp, scale=0.5))
                    else:
                        em.sop("act", lambda e: e.activation(
                            out=CHT[1][:, :], in_=CHT[1][:, :], func=AF.Exp,
                            scale=0.5))
                        tq = em.sop("act", lambda e, J=J: e.activation(
                            out=NK0[:, J * 512:(J + 1) * 512], in_=CHT[1][:, :],
                            func=AF.Identity, scale=-1.0))
                    pb6[0] = tq
                    chain_free[1] = tq
                    rope_toks += [trr, tq, tsq]
                up_slot_free[0] = [blocks[0], t12]
                up_slot_free[1] = [blocks[1], t12]
            def chain_h1(ctx):
                pp, z, jk0, Jq2 = ctx["pp"], ctx["z"], ctx["jk0"], ctx["Jq"]
                em.wait("dve", [ctx["tsc"], chain_free[pp]])
                em.sop("dve", lambda e, pp=pp, z=z, ninv=ctx["ninv"]:
                       e.tensor_scalar(out=z, in0=PBP[pp][:, :], scalar1=ninv,
                                       scalar2=1.0 + ATT_EPS,
                                       op0=ALU.mult, op1=ALU.max))
                tsq = em.sop("dve", lambda e, pp=pp, z=z: e.tensor_mul(
                    out=PBP[pp][:, :], in0=z, in1=z))
                em.wait("act", tsq)
                em.sop("act", lambda e, pp=pp: e.activation(
                    out=PBP[pp][:, :], in_=PBP[pp][:, :], func=AF.Ln,
                    bias=BIAS_NEG1))
                ctx["tu1"] = em.sop("act", lambda e, pp=pp: e.activation(
                    out=PBP[pp][:, :], in_=PBP[pp][:, :], func=AF.Exp, scale=0.5))

            def chain_h2(ctx):
                pp, z, jk0, Jq2, slot2 = (ctx["pp"], ctx["z"], ctx["jk0"],
                                          ctx["Jq"], ctx["slot"])
                em.wait("dve", ctx["tu1"])
                tu = em.sop("dve", lambda e, pp=pp, z=z: e.tensor_add(
                    out=z, in0=z, in1=PBP[pp][:, :]))
                pb_free[2 * pp] = tu
                pb_free[2 * pp + 1] = tu
                em.wait("act", tu)
                em.sop("act", lambda e, z=z: e.activation(
                    out=z, in_=z, func=AF.Ln))
                em.sop("act", lambda e, z=z, sqk=ctx["sqk"]: e.activation(
                    out=z, in_=z, func=AF.Ln, scale=sqk, bias=BIAS_ATTEPS))
                trc = em.sop("act", lambda e, z=z: e.activation(
                    out=z, in_=z, func=AF.Exp, scale=-1.0))
                em.wait("dve", trc)
                tmin = em.sop("dve", lambda e, z=z: e.tensor_scalar_min(
                    out=z, in0=z, scalar1=AW_CLAMP))
                edst = EBUF[slot2][:, jk0 * 512:(jk0 + 2) * 512]
                em.wait("act", [tmin, ebuf_free[slot2]])
                tE = em.sop("act", lambda e, z=z, edst=edst: e.activation(
                    out=edst, in_=z, func=AF.Exp))
                if jk0 >= 4 * Jq2:
                    v = jk0 - 4 * Jq2
                    em.wait("dve", [tE, tmask])
                    tE = em.sop("dve", lambda e, edst=edst, v=v: e.tensor_mul(
                        out=edst, in0=edst,
                        in1=MASKT[:, v * 512:(v + 2) * 512]))
                chain_free[pp] = tE
                ctx["epair"].append(tE)

            for s_i in range(4):
                h = 4 * g + s_i
                rb = 32 * s_i
                ninv = CONSTS[:, 16 + h:17 + h]
                sqk = CONSTS[:, 32 + h:33 + h]
                for Jq in range(NJB):
                    slot = (h * NJB + Jq) % 2
                    njk = 4 * Jq + 4
                    npair = njk // 2
                    e_pair = []
                    for p in range(npair):
                        pp = pipe_pp[0]
                        pipe_pp[0] ^= 1
                        em.wait("pe", [pb_free[2 * pp], pb_free[2 * pp + 1]]
                                + rope_toks)
                        tsc = None
                        for half in range(2):
                            jk = 2 * p + half
                            bview = PB[2 * pp + half]
                            em.op("pe", lambda e, bv=bview, rb=rb, jk=jk, Jq=Jq:
                                  e.matmul(
                                      bv,
                                      R1K[rb:rb + 32, jk * 128:(jk + 1) * 128],
                                      R1Q[rb:rb + 32, Jq * 512:(Jq + 1) * 512],
                                      start=True, stop=False,
                                      tile_position=(rb, 0)))
                            em.op("pe", lambda e, bv=bview, rb=rb, jk=jk, Jq=Jq:
                                  e.matmul(
                                      bv,
                                      R2K[rb:rb + 32, jk * 128:(jk + 1) * 128],
                                      R2Q[rb:rb + 32, Jq * 512:(Jq + 1) * 512],
                                      start=False, stop=False,
                                      tile_position=(rb, 0)))
                            tsc = em.op("pe", lambda e, bv=bview, rb=rb, jk=jk,
                                        Jq=Jq: e.matmul(
                                            bv,
                                            NK0[rb:rb + 1,
                                                jk * 128:(jk + 1) * 128],
                                            Q0B[rb:rb + 1,
                                                Jq * 512:(Jq + 1) * 512],
                                            start=False, stop=True,
                                            tile_position=(rb, 0)),
                                        inc=pe if half == 1 else None)
                        last_tsc[0] = tsc
                        ctx = dict(pp=pp, z=ZP[pp][:, :], jk0=2 * p, Jq=Jq,
                                   slot=slot, tsc=tsc, ninv=ninv, sqk=sqk,
                                   epair=e_pair)
                        chain_h1(ctx)
                        if pipe_prev[0] is not None:
                            chain_h2(pipe_prev[0])
                        pipe_prev[0] = ctx
                    if pend[0] is not None:
                        flush_attnv(pend[0])
                    pend[0] = (h, Jq, slot, njk, e_pair)
        if pipe_prev[0] is not None:
            chain_h2(pipe_prev[0])
            pipe_prev[0] = None
        if pend[0] is not None:
            flush_attnv(pend[0])
        pb_free[5] = pb5[0]
        pb_free[6] = pb6[0]
        return last_attnv[0], ot_tok[0], ebuf_free

    def emit_outproj(last_attnv, ot_tok, ebuf_free, xa_store_toks):
        wout_toks = []
        xb_store_toks = []
        for m in range(NM):
            ps = m % 2
            em.wait("sp", pool_free[ps])
            tl = dma_in(POOL[ps][:, :], woutT_r[:, m, :])
            em.wait("dve", [tl, last_attnv, ebuf_free[0], ebuf_free[1]])
            tw = em.op("dve", lambda e, ps=ps, m=m: e.tensor_copy(
                out=WOUTB[:, m * 1024:(m + 1) * 1024], in_=POOL[ps][:, :]), inc=dve)
            pool_free[ps] = tw
            wout_toks.append(tw)
        last = None
        xb_free = [None]
        for j in range(NJ):
            xp = 0 if j % 2 == 0 else 2
            em.wait("sp", [pool_free[xp]] + xa_store_toks)
            tl = dma_in(POOL[xp][:, :], xA_d[j * 128:(j + 1) * 128, :])
            em.wait("dve", [pool_free[3], xb_free[0]])
            txb = None
            for G in range(2):
                bank = (j * 2 + G) % 5
                em.wait("pe", [pb_free[bank], ot_tok] + wout_toks)
                tmm = None
                for m in range(NM):
                    tmm = em.op("pe", lambda e, bank=bank, m=m, j=j, G=G: e.matmul(
                        PB[bank],
                        OT[:, m * T + j * 128: m * T + j * 128 + 128],
                        WOUTB[:, m * 1024 + G * 512: m * 1024 + (G + 1) * 512],
                        start=(m == 0), stop=(m == NM - 1)),
                        inc=pe if m == NM - 1 else None)
                em.wait("dve", [tmm, tl])
                txb = em.op("dve", lambda e, bank=bank, G=G, xp=xp: e.tensor_add(
                    out=POOL[1][:, G * 512:(G + 1) * 512], in0=PB[bank],
                    in1=POOL[xp][:, G * 512:(G + 1) * 512]), inc=dve)
                pb_free[bank] = txb
            pool_free[xp] = txb
            em.wait("act", txb)
            tso = dma_out(xB_d[j * 128:(j + 1) * 128, :], POOL[1][:, :])
            xb_store_toks.append(tso)
            toks = [txb, tso]
            if TAP == "xB" or PHASE == 2:
                toks.append(dma_out(y_d[j * 128:(j + 1) * 128, :], POOL[1][:, :]))
            lb = LNBF if j % 2 == 0 else LNBF2
            em.wait("act", lnbf_free[j % 2])
            ta = emit_ln_rows(POOL[1][:, :], POOL[3][:, :], lb[:, :], [txb],
                              nm_store=STNM3[:, j:j + 1], r_store=STR3[:, j:j + 1],
                              alt=(j % 2 == 1))
            pool_free[3] = ta
            last = emit_transpose_chunk(j, ta, src=lb)
            toks.append(ta)
            xb_free[0] = toks
        lnxT_tok[0] = last
        return xb_store_toks

    if PHASE == -1:
        em.wait("act", tok_consts)
        dma_out(y_d[0:128, 0:64], CONSTS[:, :])
    if PHASE >= 0:
        em.scope("p_ln1")
        emit_ln_pass(x_d, STNM1, STR1)
        em.escope()
    em.scope("p_gate1")
    sp1 = (emit_gate() if not LN_ONLY else lnxT_tok[0]) if PHASE >= 0 else None
    em.escope()
    if PHASE == 0:
        em.wait("act", sp1)
        em.wait("act", lnxT_tok[0])
        dma_out(y_d[0:128, 0:128], SPARSE[:, :])
        scr_tok = None
    elif PHASE >= 1:
        em.scope("p_ffn1")
        scr_tok = emit_ffn(True, sp1)
        em.escope()
    else:
        scr_tok = None
    if PHASE >= 1:
        em.scope("p_combA")
        xa_toks = emit_combine(x_d, STNM1, STR1, xA_d,
                               TAP == "xA" or PHASE == 1, True, None, None)
        em.escope()
    if PHASE >= 2:
        em.scope("p_attn")
        la, ot, ebf = emit_attention()
        em.escope()
        em.scope("p_outproj")
        xb_toks = emit_outproj(la, ot, ebf, xa_toks)
        em.escope()
    if PHASE >= 3:
        em.scope("p_gate2")
        sp2 = emit_gate()
        em.escope()
        em.scope("p_ffn2")
        emit_ffn(False, sp2, scr_wait=[scr_tok, lnxT_tok[0]])
        em.escope()
        em.scope("p_combC")
        emit_combine(xB_d, STNM3, STR3, y_d, False, False, None, None,
                     x_wait=xb_toks)
        em.escope()

    for s in ST_SEMS:
        if em.cnt.get(s, 0):
            em.q["act"].append(("w", s, em.cnt[s]))
    for s in LD_SEMS:
        if em.cnt.get(s, 0):
            em.q["sp"].append(("w", s, em.cnt[s]))

    from contextlib import ExitStack
    with ExitStack() as stack:
        for s in LD_SEMS + ST_SEMS + ["pe", "dve", "act", "gp"]:
            em.hw[s] = stack.enter_context(nc.semaphore(s))
        block = stack.enter_context(nc.Block())

        @block.sync
        def _(e):
            em.install(e, "sp", nc)

        @block.tensor
        def _(e):
            em.install(e, "pe", nc)

        @block.vector
        def _(e):
            em.install(e, "dve", nc)

        @block.scalar
        def _(e):
            em.install(e, "act", nc)

        @block.gpsimd
        def _(e):
            em.install(e, "gp", nc)

    return nc


def _host_constants():
    inv_freq = 1.0 / (10000.0 ** (np.arange(0, HD, 2, dtype=np.float64) / HD))
    pos = np.arange(T, dtype=np.float64)
    freqs = pos[:, None] * inv_freq[None, :]
    cos = np.cos(freqs).astype(np.float32)
    sin = np.sin(freqs).astype(np.float32)
    crep = np.empty((128, T), np.float32)
    srep = np.empty((128, T), np.float32)
    for p in range(128):
        crep[p] = cos[:, p % 32]
        srep[p] = sin[:, p % 32]
    maskT = np.zeros((128, 4, 512), np.float32)
    for v in range(4):
        for p in range(128):
            maskT[p, v, p + 128 * v:] = 1.0
    iden = np.eye(128, dtype=np.float32)
    blkd = np.zeros((128, 128), np.float32)
    for p in range(128):
        blkd[p, 32 * (p // 32)] = 1.0
    bf = ml_dtypes.bfloat16
    return (crep.astype(bf), srep.astype(bf), maskT.reshape(128, 2048).astype(bf),
            iden.astype(bf), blkd.astype(bf), np.ones((128, 128), np.float32))


def kernel(L0_attn_x, L0_attn_W_qkv, L0_attn_W_out, L0_attn_hyp_k,
           L0_ffn_gate_w, L0_ffn_expert_bias, L0_ffn_w_up, L0_ffn_w_down):
    x = np.ascontiguousarray(np.asarray(L0_attn_x, np.float32))
    wupT = np.ascontiguousarray(
        np.asarray(L0_ffn_w_up, np.float32).transpose(0, 2, 1))
    wdnT = np.ascontiguousarray(
        np.asarray(L0_ffn_w_down, np.float32).transpose(0, 2, 1))
    wqkvT = np.ascontiguousarray(np.asarray(L0_attn_W_qkv, np.float32).T)
    woutT = np.ascontiguousarray(np.asarray(L0_attn_W_out, np.float32).T)
    gwT = np.ascontiguousarray(np.asarray(L0_ffn_gate_w, np.float32).T)
    eb = np.asarray(L0_ffn_expert_bias, np.float32).reshape(1, E)
    hyp = np.asarray(L0_attn_hyp_k, np.float32).reshape(1, NH)
    kcmap = np.empty((128, 4), np.float32)
    for g in range(4):
        for p in range(128):
            kcmap[p, g] = hyp[0, 4 * g + p // 32]
    crep, srep, maskT, iden, blkd, onesm = _host_constants()

    nc = build_nc(TAP)
    shared = dict(wupT=wupT, wdnT=wdnT, wqkvT=wqkvT, woutT=woutT, gwT=gwT,
                  eb=eb, hyp=hyp, kcmap=kcmap, crep=crep, srep=srep,
                  maskT=maskT, iden=iden, blkd=blkd, onesm=onesm)
    in_maps = [dict(shared, x=np.ascontiguousarray(x[b])) for b in range(8)]
    res = run_bass_kernel_spmd(nc, in_maps, core_ids=list(range(8)))
    return np.stack([r["y"] for r in res.results], axis=0)


if __name__ == "__main__":
    nc = build_nc()
    counts = {e: len([i for i in q if i[0] == "i"]) for e, q in [(None, [])]} \
        if False else None
    print("built ok")


# version 11
# speedup vs baseline: 1.0786x; 1.0786x over previous
import numpy as np
import ml_dtypes
import concourse.bass as bass
import concourse.mybir as mybir
from concourse.bass_utils import run_bass_kernel_spmd

F32 = mybir.dt.float32
BF16 = mybir.dt.bfloat16
AF = mybir.ActivationFunctionType
ALU = mybir.AluOpType
AX = mybir.AxisListType.X

T, C, II, E, NH, HD = 2048, 1024, 4096, 8, 16, 64
NJ, NM, NJB = 16, 8, 4
LN_EPS = 1e-5
GATE_EPS = 1e-6
ATT_EPS = 1e-6
AW_CLAMP = 60.0

TAP = None
PHASE = 3
N_EXP = E
LN_ONLY = False
LN_LVL = 3
NJL = NJ


class Em:
    ENGS = ("sp", "pe", "dve", "act", "gp")

    def __init__(self):
        self.q = {e: [] for e in self.ENGS}
        self.cnt = {}
        self.waited = {e: {} for e in self.ENGS}
        self.hw = {}

    def wait(self, eng, tok):
        if tok is None:
            return
        if isinstance(tok, list):
            for t in tok:
                self.wait(eng, t)
            return
        sem, val = tok
        if self.waited[eng].get(sem, 0) >= val:
            return
        self.waited[eng][sem] = val
        self.q[eng].append(("w", sem, val))

    def scope(self, name):
        for e in self.ENGS:
            self.q[e].append(("s", name))

    def escope(self):
        for e in self.ENGS:
            self.q[e].append(("e",))

    def sop(self, eng, make):
        self.wait(eng, (eng, self.cnt.get(eng, 0)))
        return self.op(eng, make, inc=eng)

    def op(self, eng, make, inc=None, n=1):
        tok = None
        if inc is not None:
            self.cnt[inc] = self.cnt.get(inc, 0) + n
            tok = (inc, self.cnt[inc])
        self.q[eng].append(("i", make, inc, n))
        return tok

    def install(self, engine, eng_name, nc=None):
        hw = self.hw
        sstack = []
        for item in self.q[eng_name]:
            if item[0] == "w":
                engine.wait_ge(hw[item[1]], item[2])
            elif item[0] == "s":
                if nc is not None:
                    sid, _ = nc.enter_named_scope(item[1], False)
                    sstack.append((item[1], sid))
            elif item[0] == "e":
                if nc is not None and sstack:
                    nm, sid = sstack.pop()
                    nc.leave_named_scope(nm, sid, False)
            else:
                _, make, inc, n = item
                inst = make(engine)
                if inc is not None:
                    inst.then_inc(hw[inc], n)


def build_nc(tap=None):
    nc = bass.Bass("TRN2", target_bir_lowering=False,
                   detect_race_conditions=False)

    x_d = nc.dram_tensor("x", [T, C], F32, kind="ExternalInput").ap()
    wupT_d = nc.dram_tensor("wupT", [E, C, II], F32, kind="ExternalInput").ap()
    wdnT_d = nc.dram_tensor("wdnT", [E, II, C], F32, kind="ExternalInput").ap()
    wqkvT_d = nc.dram_tensor("wqkvT", [C, 3 * C], F32, kind="ExternalInput").ap()
    woutT_d = nc.dram_tensor("woutT", [C, C], F32, kind="ExternalInput").ap()
    gwT_d = nc.dram_tensor("gwT", [C, E], F32, kind="ExternalInput").ap()
    eb_d = nc.dram_tensor("eb", [1, E], F32, kind="ExternalInput").ap()
    hyp_d = nc.dram_tensor("hyp", [1, NH], F32, kind="ExternalInput").ap()
    kcmap_d = nc.dram_tensor("kcmap", [128, 4], F32, kind="ExternalInput").ap()
    crep_d = nc.dram_tensor("crep", [128, T], BF16, kind="ExternalInput").ap()
    srep_d = nc.dram_tensor("srep", [128, T], BF16, kind="ExternalInput").ap()
    maskT_d = nc.dram_tensor("maskT", [128, 4 * 512], BF16, kind="ExternalInput").ap()
    iden_d = nc.dram_tensor("iden", [128, 128], BF16, kind="ExternalInput").ap()
    blkd_d = nc.dram_tensor("blkd", [128, 128], BF16, kind="ExternalInput").ap()
    ones_d = nc.dram_tensor("onesm", [128, 128], F32, kind="ExternalInput").ap()

    y_d = nc.dram_tensor("y", [T, C], F32, kind="ExternalOutput").ap()
    xA_d = nc.dram_tensor("xA", [T, C], F32, kind="Internal").ap()
    xB_d = nc.dram_tensor("xB", [T, C], F32, kind="Internal").ap()
    scrup_d = nc.dram_tensor("scrup", [E, 2 * NJ, 128, 1024], BF16,
                             kind="Internal").ap()
    scrdn_d = nc.dram_tensor("scrdn", [E, 2, 2, 128, 8192], BF16,
                             kind="Internal").ap()

    wupT_r = [wupT_d[e].rearrange("(m p) i -> p m i", p=128) for e in range(E)]
    wdnT_r = [wdnT_d[e].rearrange("(n p) h -> p n h", p=128) for e in range(E)]
    wqkvT_r = wqkvT_d.rearrange("(m p) f -> p m f", p=128)
    wqkvT_r4 = wqkvT_d.rearrange("(m p) (h d) -> p m h d", p=128, d=HD)
    woutT_r = woutT_d.rearrange("(m p) c -> p m c", p=128)
    gwT_r = gwT_d.rearrange("(m p) e -> p m e", p=128)

    SB_BASE = 16896

    def SBA(name, shape, dtype, offset):
        return nc.alloc_sbuf_tensor_at(name, shape, dtype, offset=offset + SB_BASE)
    lnxT = SBA("lnxT", [128, NM * T], BF16, offset=0)
    B0 = 32768
    OM = SBA("OM", [128, NJ * C], BF16, offset=B0)
    EBUF = [SBA(f"ebuf{i}", [128, 16 * 512], BF16, offset=B0 + 16384 * i)
            for i in range(2)]
    WVBF = SBA("wvbf", [128, NM * 1024], BF16, offset=B0)
    WOUTB = SBA("woutb", [128, NM * 1024], BF16, offset=B0)
    C0 = 65536
    HT = SBA("HT", [128, 16 * T], BF16, offset=C0)
    VSB = SBA("vsb", [128, NJ * 1056], BF16, offset=C0)
    R1Q = SBA("r1q", [128, T], BF16, offset=C0 + 33792)
    R2Q = SBA("r2q", [128, T], BF16, offset=C0 + 33792 + 4096)
    R1K = SBA("r1k", [128, T], BF16, offset=C0 + 33792 + 8192)
    R2K = SBA("r2k", [128, T], BF16, offset=C0 + 33792 + 12288)
    NK0 = SBA("nk0", [128, T], BF16, offset=C0 + 50176)
    Q0B = SBA("q0b", [128, T], BF16, offset=C0 + 54272)
    MASKT = SBA("maskt", [128, 4 * 512], BF16, offset=C0 + 58368)
    D0 = 131072
    WDN = [SBA(f"wdn{i}", [128, 16 * 512], BF16, offset=D0 + 16384 * i)
           for i in range(3)]
    OT = SBA("oT", [128, NM * T], BF16, offset=D0)
    CHZ = [SBA(f"chz{i}", [128, 512], F32, offset=D0 + 32768 + 2048 * i)
           for i in range(2)]
    ZP = [SBA(f"zp{i}", [128, 1024], F32, offset=D0 + 32768 + 4096 * i)
          for i in range(2)]
    ZP1A = ZP[1].ap()[:, :]
    CHT = [SBA(f"cht{i}", [128, 512], F32, offset=D0 + 36864 + 2048 * i)
           for i in range(2)]
    LNBF = SBA("lnbf", [128, C], BF16, offset=D0 + 32768)
    LNBF2 = SBA("lnbf2", [128, C], BF16, offset=D0 + 34816)
    SQS = SBA("sqs", [128, 512], BF16, offset=D0 + 40960)
    SQS2 = SBA("sqs2", [128, 512], BF16, offset=D0 + 41984)
    REPR = SBA("repr", [128, 512], F32, offset=D0 + 40960)
    SSB = SBA("ssb", [128, 512], F32, offset=D0 + 43008)
    WQK = [SBA(f"wqk{i}", [128, 1024], BF16, offset=D0 + 45056 + 2048 * i)
           for i in range(2)]
    WUPS = [SBA("wups0", [128, 1024], BF16, offset=180224 + 16384),
            SBA("wups1", [128, 1024], BF16, offset=198656 + 11104)]
    E0 = 180224
    POOL = [SBA(f"pool{i}", [128, 1024], F32, offset=E0 + 4096 * i) for i in range(4)]
    F0 = 198656
    CREPS = SBA("creps", [128, T], BF16, offset=F0)
    SREPS = SBA("sreps", [128, T], BF16, offset=F0 + 4096)
    IDEN = SBA("iden_s", [128, 128], BF16, offset=F0 + 8192)
    BLKD = SBA("blkd_s", [128, 128], BF16, offset=F0 + 8448)
    ONES = SBA("ones_s", [128, 128], F32, offset=F0 + 8704)
    CONSTS = SBA("consts", [128, 64], F32, offset=F0 + 9216)
    KCMAP = SBA("kcmap_s", [128, 4], F32, offset=F0 + 9472)
    HROW = SBA("hrow", [128, 64], F32, offset=F0 + 9504)
    EBR = SBA("ebr", [128, 8], F32, offset=F0 + 9760)
    GWB = SBA("gwb", [128, 64], BF16, offset=F0 + 9792)
    SPARSE = SBA("sparse", [128, NJ * E], F32, offset=F0 + 9920)
    STNM1 = SBA("stnm1", [128, NJ], F32, offset=F0 + 10432)
    STR1 = SBA("str1", [128, NJ], F32, offset=F0 + 10496)
    STNM3 = SBA("stnm3", [128, NJ], F32, offset=F0 + 10560)
    STR3 = SBA("str3", [128, NJ], F32, offset=F0 + 10624)
    SC = [SBA(f"sc{i}", [128, 1], F32, offset=F0 + 10688 + 32 * i) for i in range(8)]
    SC2 = [SBA(f"sc2{i}", [128, 1], F32, offset=F0 + 13152 + 32 * i)
           for i in range(4)]
    GSC = SBA("gsc", [128, 32], F32, offset=F0 + 10944)
    CBIAS = SBA("cbias", [128, 4], F32, offset=F0 + 11072)

    PBP = [nc.alloc_psum_tensor(f"pbp{i}", [128, 1024], F32) for i in range(3)]
    PBS6 = nc.alloc_psum_tensor("pbs6", [128, 512], F32)
    PB = [PBP[i // 2][:, (i % 2) * 512:(i % 2) * 512 + 512] for i in range(6)]
    PB = PB + [PBS6[:, :]]
    PB5AP = PBP[2].ap()[:, 512:1024]
    PT = nc.alloc_psum_tensor("pt", [128, 1024], BF16)

    em = Em()
    pe, dve, act = "pe", "dve", "act"

    NLD, NST = 10, 6
    LD_SEMS = [f"ld{i}" for i in range(NLD)]
    ST_SEMS = [f"st{i}" for i in range(NST)]
    _ld_i = [0]
    _st_i = [0]

    def dma_in(dst, src):
        s = LD_SEMS[_ld_i[0] % NLD]
        _ld_i[0] += 1
        em.wait("sp", (s, em.cnt.get(s, 0)))
        return em.op("sp", lambda e, d=dst, s2=src: e.dma_start(out=d, in_=s2),
                     inc=s, n=16)

    def dma_out(dst, src):
        s = ST_SEMS[_st_i[0] % NST]
        _st_i[0] += 1
        em.wait("act", (s, em.cnt.get(s, 0)))
        return em.op("act", lambda e, d=dst, s2=src: e.dma_start(out=d, in_=s2),
                     inc=s, n=16)

    pool_free = [None, None, None, None]
    pt_free = [None]
    lnbf_free = [None, None]
    lnxT_tok = [None]
    pb_free = [None] * 7
    up_slot_free = [None, None]
    wdn_free = [None, None, None]
    wdn_seq = [0]

    em.op("gp", lambda e: e.memset(CBIAS[:, 0:1], LN_EPS))
    em.op("gp", lambda e: e.memset(CBIAS[:, 1:2], -1.0))
    tok_cbias = em.op("gp", lambda e: e.memset(CBIAS[:, 2:3], ATT_EPS), inc="gp")
    em.wait("act", tok_cbias)
    BIAS_LNEPS = CBIAS[:, 0:1]
    BIAS_NEG1 = CBIAS[:, 1:2]
    BIAS_ATTEPS = CBIAS[:, 2:3]

    tc = [dma_in(CREPS[:, :], crep_d), dma_in(SREPS[:, :], srep_d),
          dma_in(IDEN[:, :], iden_d), dma_in(BLKD[:, :], blkd_d),
          dma_in(ONES[:, :], ones_d), dma_in(KCMAP[:, :], kcmap_d),
          dma_in(HROW[0:1, 0:NH], hyp_d), dma_in(EBR[0:1, :], eb_d)]
    t_gw = dma_in(POOL[3][:, 0:64], gwT_r[:, :, :])
    em.wait("act", tc)
    em.sop("act", lambda e: e.activation(out=HROW[0:1, 48:64], in_=HROW[0:1, 0:16],
                                         func=AF.Ln))
    em.sop("act", lambda e: e.activation(out=HROW[0:1, 32:48], in_=HROW[0:1, 48:64],
                                         func=AF.Exp, scale=0.5))
    t1 = em.sop("act", lambda e: e.activation(out=HROW[0:1, 16:32],
                                              in_=HROW[0:1, 48:64],
                                              func=AF.Exp, scale=-1.0))
    em.wait("dve", t1)
    t2 = em.op("dve", lambda e: e.tensor_scalar_mul(
        out=HROW[0:1, 16:32], in0=HROW[0:1, 16:32], scalar1=-1.0), inc=dve)
    em.wait("pe", [t1, t2] + tc)
    em.op("pe", lambda e: e.matmul(PBP[0][:, 0:48], ONES[0:1, 0:128],
                                   HROW[0:1, 0:48], start=True, stop=True))
    t4 = em.op("pe", lambda e: e.matmul(PBP[0][:, 64:72], ONES[0:1, 0:128],
                                        EBR[0:1, :], start=True, stop=True), inc=pe)
    em.wait("dve", [t4, t_gw])
    em.op("dve", lambda e: e.tensor_copy(out=CONSTS[:, 0:48], in_=PBP[0][:, 0:48]))
    em.op("dve", lambda e: e.tensor_copy(out=EBR[:, :], in_=PBP[0][:, 64:72]))
    tok_consts = em.op("dve", lambda e: e.tensor_copy(out=GWB[:, :],
                                                      in_=POOL[3][:, 0:64]), inc=dve)
    pb_free[0] = tok_consts
    pool_free[3] = tok_consts

    def emit_ln_rows(src_ap, xsq_ap, out_ap, wait_toks, nm_store=None, r_store=None,
                     alt=False):
        sset = SC2 if alt else SC
        nm, r, v1, v2 = sset[0][:, :], sset[1][:, :], sset[2][:, :], sset[3][:, :]
        em.wait("dve", wait_toks)
        if LN_LVL == 1:
            return em.sop("act", lambda e: e.activation(out=out_ap, in_=src_ap,
                                                        func=AF.Identity))
        em.sop("dve", lambda e: e.reduce_sum(out=nm, in_=src_ap, axis=AX, negate=True))
        em.sop("dve", lambda e: e.tensor_mul(out=xsq_ap, in0=src_ap, in1=src_ap))
        em.sop("dve", lambda e: e.reduce_sum(out=v1, in_=xsq_ap, axis=AX))
        em.sop("dve", lambda e: e.tensor_scalar_mul(out=nm, in0=nm, scalar1=1.0 / C))
        em.sop("dve", lambda e: e.tensor_scalar_mul(out=v1, in0=v1, scalar1=1.0 / C))
        em.sop("dve", lambda e: e.tensor_mul(out=v2, in0=nm, in1=nm))
        tv = em.sop("dve", lambda e: e.tensor_sub(out=v1, in0=v1, in1=v2))
        if LN_LVL == 2:
            em.wait("act", tv)
            return em.sop("act", lambda e: e.activation(out=out_ap, in_=src_ap,
                                                        func=AF.Identity))
        em.wait("act", tv)
        em.sop("act", lambda e: e.activation(out=v2, in_=v1, func=AF.Ln,
                                             bias=BIAS_LNEPS))
        tr = em.sop("act", lambda e: e.activation(out=r, in_=v2, func=AF.Exp,
                                                  scale=-0.5))
        em.wait("dve", tr)
        tb = em.sop("dve", lambda e: e.tensor_mul(out=v2, in0=nm, in1=r))
        if nm_store is not None:
            em.sop("dve", lambda e, d=nm_store: e.tensor_copy(out=d, in_=nm))
            em.sop("dve", lambda e, d=r_store: e.tensor_copy(out=d, in_=r))
        em.wait("act", tb)
        ta = em.sop("act", lambda e: e.activation(out=out_ap, in_=src_ap,
                                                  func=AF.Identity,
                                                  bias=v2, scale=r))
        return ta

    def emit_transpose_chunk(j, wait_tok, src=None):
        buf = LNBF if src is None else src
        em.wait("pe", [wait_tok, pt_free[0]])
        tt = None
        for m in range(NM):
            tt = em.op("pe", lambda e, m=m, buf=buf: e.transpose(
                PT[:, m * 128:(m + 1) * 128], buf[:, m * 128:(m + 1) * 128],
                IDEN[:, :]), inc=pe if m == NM - 1 else None)
        em.wait("dve", tt)
        te = None
        for m in range(NM):
            te = em.op("dve", lambda e, m=m, j=j: e.tensor_copy(
                out=lnxT[:, m * T + j * 128: m * T + j * 128 + 128],
                in_=PT[:, m * 128:(m + 1) * 128]),
                inc=dve if m == NM - 1 else None)
        pt_free[0] = te
        lnbf_free[0 if buf is LNBF else 1] = te
        return te

    def emit_ln_pass(src_dram, nm_store_t, r_store_t):
        last = None
        for j in range(NJL):
            slot = j % 2
            em.wait("sp", pool_free[slot])
            tl = dma_in(POOL[slot][:, :], src_dram[j * 128:(j + 1) * 128, :])
            xs = 2 + (j % 2)
            lb = LNBF if j % 2 == 0 else LNBF2
            em.wait("dve", pool_free[xs])
            em.wait("act", lnbf_free[j % 2])
            ta = emit_ln_rows(
                POOL[slot][:, :], POOL[xs][:, :], lb[:, :], [tl],
                nm_store=None if nm_store_t is None else nm_store_t[:, j:j + 1],
                r_store=None if r_store_t is None else r_store_t[:, j:j + 1],
                alt=(j % 2 == 1))
            pool_free[slot] = ta
            pool_free[xs] = ta
            last = emit_transpose_chunk(j, ta, src=lb)
        lnxT_tok[0] = last
        return last

    def emit_gate():
        em.wait("pe", [lnxT_tok[0], tok_consts, pb_free[6]])
        pb6 = [None]
        last = None
        for j in range(NJ):
            em.wait("pe", pb6[0])
            tg = None
            for m in range(NM):
                tg = em.op("pe", lambda e, m=m, j=j: e.matmul(
                    PBS6[:, 0:8],
                    lnxT[:, m * T + j * 128: m * T + j * 128 + 128],
                    GWB[:, m * 8:(m + 1) * 8],
                    start=(m == 0), stop=(m == NM - 1)),
                    inc=pe if m == NM - 1 else None)
            lg, ex, q1m, q2m = GSC[:, 0:8], GSC[:, 8:16], GSC[:, 16:24], GSC[:, 24:32]
            mx, sall, m2, den = SC[4][:, :], SC[5][:, :], SC[6][:, :], SC[7][:, :]
            em.wait("dve", tg)
            tpb = em.sop("dve", lambda e: e.tensor_add(out=lg, in0=PBS6[:, 0:8],
                                                       in1=EBR[:, :]))
            pb6[0] = tpb
            tmx = em.sop("dve", lambda e: e.reduce_max(out=mx, in_=lg, axis=AX,
                                                       negate=True))
            em.wait("act", tmx)
            tex = em.sop("act", lambda e: e.activation(out=ex, in_=lg, func=AF.Exp,
                                                       bias=mx, accum_out=sall))
            em.wait("dve", tex)
            em.sop("dve", lambda e: e.reduce_max(out=m2, in_=ex, axis=AX))
            em.sop("dve", lambda e: e.tensor_scalar(out=q1m, in0=ex, scalar1=m2,
                                                    scalar2=None, op0=ALU.is_ge))
            em.sop("dve", lambda e: e.scalar_tensor_tensor(
                out=q1m, in0=q1m, scalar=-1e30, in1=ex, op0=ALU.mult, op1=ALU.add))
            em.sop("dve", lambda e: e.reduce_max(out=m2, in_=q1m, axis=AX))
            em.sop("dve", lambda e: e.tensor_scalar(out=q2m, in0=ex, scalar1=m2,
                                                    scalar2=None, op0=ALU.is_ge))
            em.sop("dve", lambda e: e.tensor_mul(out=q2m, in0=q2m, in1=ex))
            em.sop("dve", lambda e: e.reduce_sum(out=m2, in_=q2m, axis=AX))
            tden = em.sop("dve", lambda e: e.scalar_tensor_tensor(
                out=den, in0=sall, scalar=GATE_EPS, in1=m2, op0=ALU.mult,
                op1=ALU.add))
            em.wait("act", tden)
            em.sop("act", lambda e: e.activation(out=den, in_=den, func=AF.Ln))
            trd = em.sop("act", lambda e: e.activation(out=den, in_=den, func=AF.Exp,
                                                       scale=-1.0))
            em.wait("dve", trd)
            last = em.sop("dve", lambda e, j=j: e.tensor_scalar_mul(
                out=SPARSE[:, j * 8:(j + 1) * 8], in0=q2m, scalar1=den))
        pb_free[6] = tpb
        return last

    def emit_ffn(first_pass, sparse_tok, scr_wait=None):
        gelu_last = [None]
        om_started = [False]
        scr_last = [None]
        for e_i in range(N_EXP):
            for half in range(2):
                n0 = half * 16
                for k in range(16):
                    n = n0 + k
                    bs = n % 2
                    if first_pass:
                        ps = n % 2
                        em.wait("sp", pool_free[ps])
                        tl = dma_in(POOL[ps][:, :],
                                    wupT_r[e_i][:, :, n * 128:(n + 1) * 128])
                        em.wait("dve", [tl, up_slot_free[bs]])
                        tb = em.op("dve", lambda e, ps=ps, bs=bs: e.tensor_copy(
                            out=WUPS[bs][:, :], in_=POOL[ps][:, :]), inc=dve)
                        pool_free[ps] = tb
                        em.wait("act", tb)
                        tw = dma_out(scrup_d[e_i, n], WUPS[bs][:, :])
                        scr_last[0] = tw
                        new_free = [tw]
                    else:
                        em.wait("sp", [up_slot_free[bs], scr_wait])
                        tb = dma_in(WUPS[bs][:, :], scrup_d[e_i, n])
                        new_free = []
                    last_mm = None
                    tg = None
                    for Jb in range(NJB):
                        bank = (n * NJB + Jb) % 6
                        em.wait("pe", [tb, pb_free[bank], lnxT_tok[0]])
                        for m in range(NM):
                            last_mm = em.op(
                                "pe", lambda e, bank=bank, bs=bs, m=m, Jb=Jb: e.matmul(
                                    PB[bank],
                                    WUPS[bs][:, m * 128:(m + 1) * 128],
                                    lnxT[:, m * T + Jb * 512: m * T + (Jb + 1) * 512],
                                    start=(m == 0), stop=(m == NM - 1)),
                                inc=pe if m == NM - 1 else None)
                        em.wait("act", last_mm)
                        tg = em.op("act", lambda e, bank=bank, k=k, Jb=Jb: e.activation(
                            out=HT[:, k * T + Jb * 512: k * T + (Jb + 1) * 512],
                            in_=PB[bank], func=AF.Gelu), inc=act)
                        pb_free[bank] = tg
                    up_slot_free[bs] = new_free + [last_mm]
                    gelu_last[0] = tg
                for G in range(2):
                    ws = wdn_seq[0] % 3
                    wdn_seq[0] += 1
                    fill = []
                    if first_pass:
                        for s in range(8):
                            ps2 = 2 + (s % 2)
                            em.wait("sp", pool_free[ps2])
                            tl = dma_in(
                                POOL[ps2][:, :],
                                wdnT_r[e_i][:, n0 + 2 * s:n0 + 2 * s + 2,
                                            G * 512:(G + 1) * 512])
                            em.wait("dve", [tl, wdn_free[ws]])
                            td = em.op("dve", lambda e, ps2=ps2, ws=ws, s=s:
                                       e.tensor_copy(
                                           out=WDN[ws][:, s * 1024:(s + 1) * 1024],
                                           in_=POOL[ps2][:, :]), inc=dve)
                            pool_free[ps2] = td
                            fill.append(td)
                        em.wait("act", fill[-1])
                        tw2 = dma_out(scrdn_d[e_i, half, G], WDN[ws][:, :])
                        scr_last[0] = tw2
                        new_free = [tw2]
                    else:
                        em.wait("sp", [wdn_free[ws], scr_wait])
                        tfull = dma_in(WDN[ws][:, :], scrdn_d[e_i, half, G])
                        fill = [tfull] * 8
                        new_free = []
                    last_down = None
                    for j in range(NJ):
                        bank = ((half * 2 + G) * NJ + j) % 6
                        em.wait("pe", [pb_free[bank], gelu_last[0], sparse_tok])
                        for n_loc in range(16):
                            em.wait("pe", fill[n_loc // 2])
                            last_down = em.op(
                                "pe",
                                lambda e, bank=bank, n_loc=n_loc, j=j, ws=ws: e.matmul(
                                    PB[bank],
                                    HT[:, n_loc * T + j * 128:
                                       n_loc * T + j * 128 + 128],
                                    WDN[ws][:, n_loc * 512:(n_loc + 1) * 512],
                                    start=(n_loc == 0), stop=(n_loc == 15)),
                                inc=pe if n_loc == 15 else None)
                        em.wait("dve", [last_down, sparse_tok])
                        sp_col = SPARSE[:, j * 8 + e_i: j * 8 + e_i + 1]
                        dst = OM[:, j * C + G * 512: j * C + (G + 1) * 512]
                        if e_i == 0 and half == 0:
                            tv = em.op("dve", lambda e, bank=bank, d=dst, s=sp_col:
                                       e.tensor_scalar_mul(
                                           out=d, in0=PB[bank], scalar1=s),
                                       inc=dve)
                        else:
                            tv = em.op("dve", lambda e, bank=bank, d=dst, s=sp_col:
                                       e.scalar_tensor_tensor(
                                           out=d, in0=PB[bank], scalar=s, in1=d,
                                           op0=ALU.mult, op1=ALU.add), inc=dve)
                        pb_free[bank] = tv
                    wdn_free[ws] = new_free + [last_down]
        return scr_last[0]

    def emit_combine(x_src, nm_t, r_t, out_dram, also_y, ln_next, nm_n, r_n,
                     x_wait=None):
        store_toks = []
        lnom_free = [None]
        last = None
        for j in range(NJ):
            xp = j % 2
            xs = 2 + (j % 2)
            em.wait("sp", [pool_free[xp], x_wait])
            tl = dma_in(POOL[xp][:, :], x_src[j * 128:(j + 1) * 128, :])
            om_j = OM[:, j * C:(j + 1) * C]
            em.wait("dve", pool_free[xs])
            em.wait("act", lnom_free[0])
            tln = emit_ln_rows(om_j, POOL[xs][:, :], ZP1A, [],
                               alt=(j % 2 == 1))
            a_c, b_c = (SC[4][:, :], SC[5][:, :]) if j % 2 == 0 else (
                SC[6][:, :], SC[7][:, :])
            em.wait("dve", tln)
            em.sop("dve", lambda e, j=j: e.tensor_scalar(
                out=a_c, in0=r_t[:, j:j + 1], scalar1=0.5, scalar2=1.0,
                op0=ALU.mult, op1=ALU.add))
            em.sop("dve", lambda e, j=j: e.tensor_mul(
                out=b_c, in0=nm_t[:, j:j + 1], in1=r_t[:, j:j + 1]))
            tab = em.sop("dve", lambda e: e.tensor_scalar_mul(
                out=b_c, in0=b_c, scalar1=0.5))
            em.wait("act", [tl, tab])
            tx = em.op("act", lambda e, xp=xp: e.activation(
                out=POOL[xp][:, :], in_=POOL[xp][:, :], func=AF.Identity,
                scale=a_c, bias=b_c), inc=act)
            em.wait("dve", tx)
            txa = em.op("dve", lambda e, xp=xp: e.scalar_tensor_tensor(
                out=POOL[xp][:, :], in0=ZP1A, scalar=0.5, in1=POOL[xp][:, :],
                op0=ALU.mult, op1=ALU.add), inc=dve)
            lnom_free[0] = txa
            em.wait("act", txa)
            tso = dma_out(out_dram[j * 128:(j + 1) * 128, :], POOL[xp][:, :])
            store_toks.append(tso)
            toks = [txa, tso]
            if also_y:
                toks.append(dma_out(y_d[j * 128:(j + 1) * 128, :], POOL[xp][:, :]))
            if ln_next:
                lb = LNBF if j % 2 == 0 else LNBF2
                em.wait("act", lnbf_free[j % 2])
                ta = emit_ln_rows(
                    POOL[xp][:, :], POOL[xs][:, :], lb[:, :], [txa],
                    nm_store=None if nm_n is None else nm_n[:, j:j + 1],
                    r_store=None if r_n is None else r_n[:, j:j + 1],
                    alt=(j % 2 == 0))
                pool_free[xs] = ta
                last = emit_transpose_chunk(j, ta, src=lb)
                toks.append(ta)
            else:
                pool_free[xs] = tln
            pool_free[xp] = toks
        if ln_next:
            lnxT_tok[0] = last
        return store_toks

    def emit_attention():
        tmask = dma_in(MASKT[:, :], maskT_d)
        wv_toks = []
        for m in range(NM):
            for G in range(2):
                ps = (m * 2 + G) % 2
                em.wait("sp", pool_free[ps])
                tl = dma_in(POOL[ps][:, 0:512],
                            wqkvT_r[:, m, 2 * C + G * 512: 2 * C + (G + 1) * 512])
                em.wait("dve", [tl, lnxT_tok[0]])
                td = em.op("dve", lambda e, ps=ps, m=m, G=G: e.tensor_copy(
                    out=WVBF[:, m * 1024 + G * 512: m * 1024 + (G + 1) * 512],
                    in_=POOL[ps][:, 0:512]), inc=dve)
                pool_free[ps] = td
                wv_toks.append(td)
        em.wait("gp", lnxT_tok[0])
        tones = em.op("gp", lambda e: e.memset(VSB[:, :], 1.0), inc="gp")
        vsb_toks = []
        for j in range(NJ):
            for G in range(2):
                bank = (j * 2 + G) % 6
                em.wait("pe", [pb_free[bank], lnxT_tok[0]] + wv_toks)
                tmm = None
                for m in range(NM):
                    tmm = em.op("pe", lambda e, bank=bank, m=m, j=j, G=G: e.matmul(
                        PB[bank],
                        lnxT[:, m * T + j * 128: m * T + j * 128 + 128],
                        WVBF[:, m * 1024 + G * 512: m * 1024 + (G + 1) * 512],
                        start=(m == 0), stop=(m == NM - 1)),
                        inc=pe if m == NM - 1 else None)
                em.wait("dve", [tmm, tones])
                dst = bass.AP(VSB, j * 1056 + G * 8 * 66,
                              [[NJ * 1056, 128], [66, 8], [1, 64]])
                tv = em.op("dve", lambda e, bank=bank, d=dst: e.tensor_copy(
                    out=d, in_=PB[bank]), inc=dve)
                pb_free[bank] = tv
                vsb_toks.append(tv)
        ebuf_free = [None, None]
        chain_free = [None, None]
        pb5 = [pb_free[5]]
        pb6 = [pb_free[6]]
        last_attnv = [None]
        ot_tok = [None]
        last_tsc = [None]
        qk_free = [[lnxT_tok[0]], [lnxT_tok[0]]]
        pend = [None]
        pipe_prev = [None]
        pipe_pp = [0]

        def flush_attnv(item):
            if pipe_prev[0] is not None and pipe_prev[0]["epair"] is item[4]:
                chain_h2(pipe_prev[0])
                pipe_prev[0] = None
            h2, Jq2, slot2, njk2, e_pair2 = item
            voff2 = 64 * (h2 % 2)
            tav = None
            for jk in range(njk2):
                em.wait("pe", [e_pair2[jk // 2], pb5[0], pb6[0]] + vsb_toks)
                vsl = VSB[:, jk * 1056 + h2 * 66: jk * 1056 + h2 * 66 + 64]
                osl = VSB[:, jk * 1056 + h2 * 66 + 64: jk * 1056 + h2 * 66 + 65]
                esl = EBUF[slot2][:, jk * 512:(jk + 1) * 512]
                em.op("pe", lambda e, vsl=vsl, esl=esl, voff=voff2, jk=jk,
                      njk=njk2: e.matmul(PB5AP[voff:voff + 64, :], vsl, esl,
                                         start=(jk == 0), stop=(jk == njk - 1),
                                         tile_position=(0, voff)))
                tav = em.op("pe", lambda e, osl=osl, esl=esl, jk=jk, njk=njk2:
                            e.matmul(PBS6[64:65, :], osl, esl,
                                     start=(jk == 0), stop=(jk == njk - 1),
                                     tile_position=(0, 64)),
                            inc=pe if jk == njk2 - 1 else None)
            ebuf_free[slot2] = tav
            last_attnv[0] = tav
            em.wait("dve", tav)
            tss = em.op("dve", lambda e: e.tensor_copy(
                out=SSB[64:65, :], in_=PBS6[64:65, :]), inc=dve)
            em.wait("pe", [tss])
            trep = em.op("pe", lambda e, voff=voff2: e.matmul(
                PBS6[voff:voff + 64, :], ONES[64:65, 0:64],
                SSB[64:65, :], start=True, stop=True,
                tile_position=(64, voff)), inc=pe)
            em.wait("act", trep)
            tl6 = em.sop("act", lambda e, voff=voff2: e.activation(
                out=REPR[voff:voff + 64, :], in_=PBS6[voff:voff + 64, :],
                func=AF.Ln))
            trc2 = em.sop("act", lambda e, voff=voff2: e.activation(
                out=REPR[voff:voff + 64, :], in_=REPR[voff:voff + 64, :],
                func=AF.Exp, scale=-1.0))
            pb6[0] = tl6
            odst = OT[voff2:voff2 + 64,
                      (h2 // 2) * T + Jq2 * 512:(h2 // 2) * T + (Jq2 + 1) * 512]
            em.wait("dve", trc2)
            tot = em.op("dve", lambda e, odst=odst, voff=voff2: e.tensor_mul(
                out=odst, in0=PB5AP[voff:voff + 64, :],
                in1=REPR[voff:voff + 64, :]), inc=dve)
            pb5[0] = tot
            ot_tok[0] = tot

        for g in range(4):
            if pipe_prev[0] is not None:
                chain_h2(pipe_prev[0])
                pipe_prev[0] = None
            rope_toks = []
            for side in range(2):
                blocks = []
                for hf in range(2):
                    bs = hf
                    ps = 2 + hf
                    em.wait("sp", [pool_free[ps], qk_free[bs]])
                    hbase = side * NH + 4 * g
                    tl = []
                    for hh in range(4):
                        src = wqkvT_r4[:, :, hbase + hh, hf * 32:(hf + 1) * 32]
                        dsth = bass.AP(POOL[ps], hh * 32,
                                       [[1024, 128], [128, 8], [1, 32]])
                        tl.append(dma_in(dsth, src))
                    em.wait("dve", tl)
                    tb = em.op("dve", lambda e, ps=ps, bs=bs: e.tensor_copy(
                        out=WQK[bs][:, :], in_=POOL[ps][:, :]), inc=dve)
                    pool_free[ps] = tb
                    blocks.append(tb)
                R1 = R1Q if side == 0 else R1K
                R2 = R2Q if side == 0 else R2K
                t12 = None
                for J in range(NJB):
                    b1, b2 = (J * 2) % 4, (J * 2 + 1) % 4
                    em.wait("pe", [blocks[0], blocks[1], pb_free[b1], pb_free[b2],
                                   lnxT_tok[0]])
                    for m in range(NM):
                        em.op("pe", lambda e, b1=b1, m=m, J=J: e.matmul(
                            PB[b1][:, :], WQK[0][:, m * 128:(m + 1) * 128],
                            lnxT[:, m * T + J * 512: m * T + (J + 1) * 512],
                            start=(m == 0), stop=(m == NM - 1)))
                    for m in range(NM):
                        t12 = em.op("pe", lambda e, b2=b2, m=m, J=J: e.matmul(
                            PB[b2][:, :], WQK[1][:, m * 128:(m + 1) * 128],
                            lnxT[:, m * T + J * 512: m * T + (J + 1) * 512],
                            start=(m == 0), stop=(m == NM - 1)),
                            inc=pe if m == NM - 1 else None)
                    cs = CREPS[:, J * 512:(J + 1) * 512]
                    sn = SREPS[:, J * 512:(J + 1) * 512]
                    r1sl = R1[:, J * 512:(J + 1) * 512]
                    r2sl = R2[:, J * 512:(J + 1) * 512]
                    em.wait("dve", [t12, chain_free[0], chain_free[1],
                                    last_tsc[0]])
                    em.sop("dve", lambda e, b1=b1, cs=cs: e.tensor_mul(
                        out=CHZ[0][:, :], in0=PB[b1][:, :], in1=cs))
                    em.sop("dve", lambda e, b2=b2, sn=sn: e.tensor_mul(
                        out=CHT[0][:, :], in0=PB[b2][:, :], in1=sn))
                    em.sop("dve", lambda e, r1sl=r1sl: e.tensor_add(
                        out=r1sl, in0=CHZ[0][:, :], in1=CHT[0][:, :]))
                    em.sop("dve", lambda e, b2=b2, cs=cs: e.tensor_mul(
                        out=CHZ[0][:, :], in0=PB[b2][:, :], in1=cs))
                    em.sop("dve", lambda e, b1=b1, sn=sn: e.tensor_mul(
                        out=CHT[0][:, :], in0=PB[b1][:, :], in1=sn))
                    trr = em.sop("dve", lambda e, r2sl=r2sl: e.tensor_sub(
                        out=r2sl, in0=CHZ[0][:, :], in1=CHT[0][:, :]))
                    pb_free[b1] = trr
                    pb_free[b2] = trr
                    chain_free[0] = trr
                    chain_free[1] = trr
                    em.sop("dve", lambda e, r1sl=r1sl: e.tensor_mul(
                        out=SQS[:, :], in0=r1sl, in1=r1sl))
                    em.sop("dve", lambda e, r2sl=r2sl: e.tensor_mul(
                        out=SQS2[:, :], in0=r2sl, in1=r2sl))
                    tsq = em.sop("dve", lambda e: e.tensor_add(
                        out=SQS[:, :], in0=SQS[:, :], in1=SQS2[:, :]))
                    em.wait("pe", [tsq, pb6[0]])
                    tq0 = em.op("pe", lambda e: e.matmul(
                        PBS6[:, :], BLKD[:, :], SQS[:, :], start=True, stop=True),
                        inc=pe)
                    em.wait("act", [tq0, chain_free[1], last_tsc[0]])
                    kcc = KCMAP[:, g:g + 1]
                    em.sop("act", lambda e, kcc=kcc: e.activation(
                        out=CHT[1][:, :], in_=PBS6[:, :], func=AF.Ln, bias=kcc))
                    if side == 0:
                        tq = em.sop("act", lambda e, J=J: e.activation(
                            out=Q0B[:, J * 512:(J + 1) * 512], in_=CHT[1][:, :],
                            func=AF.Exp, scale=0.5))
                    else:
                        em.sop("act", lambda e: e.activation(
                            out=CHT[1][:, :], in_=CHT[1][:, :], func=AF.Exp,
                            scale=0.5))
                        tq = em.sop("act", lambda e, J=J: e.activation(
                            out=NK0[:, J * 512:(J + 1) * 512], in_=CHT[1][:, :],
                            func=AF.Identity, scale=-1.0))
                    pb6[0] = tq
                    chain_free[1] = tq
                    rope_toks += [trr, tq, tsq]
                up_slot_free[0] = [blocks[0], t12]
                up_slot_free[1] = [blocks[1], t12]
            def chain_h1(ctx):
                pp, z, jk0, Jq2 = ctx["pp"], ctx["z"], ctx["jk0"], ctx["Jq"]
                em.wait("dve", [ctx["tsc"], chain_free[pp]])
                em.sop("dve", lambda e, pp=pp, z=z, ninv=ctx["ninv"]:
                       e.tensor_scalar(out=z, in0=PBP[pp][:, :], scalar1=ninv,
                                       scalar2=1.0 + ATT_EPS,
                                       op0=ALU.mult, op1=ALU.max))
                tsq = em.sop("dve", lambda e, pp=pp, z=z: e.tensor_mul(
                    out=PBP[pp][:, :], in0=z, in1=z))
                em.wait("act", tsq)
                em.sop("act", lambda e, pp=pp: e.activation(
                    out=PBP[pp][:, :], in_=PBP[pp][:, :], func=AF.Ln,
                    bias=BIAS_NEG1))
                ctx["tu1"] = em.sop("act", lambda e, pp=pp: e.activation(
                    out=PBP[pp][:, :], in_=PBP[pp][:, :], func=AF.Exp, scale=0.5))

            def chain_h2a(ctx):
                pp, z = ctx["pp"], ctx["z"]
                em.wait("dve", ctx["tu1"])
                tu = em.sop("dve", lambda e, pp=pp, z=z: e.tensor_add(
                    out=z, in0=z, in1=PBP[pp][:, :]))
                pb_free[2 * pp] = tu
                pb_free[2 * pp + 1] = tu
                em.wait("act", tu)
                em.sop("act", lambda e, z=z: e.activation(
                    out=z, in_=z, func=AF.Ln))
                em.sop("act", lambda e, z=z, sqk=ctx["sqk"]: e.activation(
                    out=z, in_=z, func=AF.Ln, scale=sqk, bias=BIAS_ATTEPS))
                ctx["trc"] = em.sop("act", lambda e, z=z: e.activation(
                    out=z, in_=z, func=AF.Exp, scale=-1.0))

            def chain_h2b(ctx):
                pp, z, jk0, Jq2, slot2 = (ctx["pp"], ctx["z"], ctx["jk0"],
                                          ctx["Jq"], ctx["slot"])
                em.wait("dve", ctx["trc"])
                tmin = em.sop("dve", lambda e, z=z: e.tensor_scalar_min(
                    out=z, in0=z, scalar1=AW_CLAMP))
                edst = EBUF[slot2][:, jk0 * 512:(jk0 + 2) * 512]
                em.wait("act", [tmin, ebuf_free[slot2]])
                tE = em.sop("act", lambda e, z=z, edst=edst: e.activation(
                    out=edst, in_=z, func=AF.Exp))
                if jk0 >= 4 * Jq2:
                    v = jk0 - 4 * Jq2
                    em.wait("dve", [tE, tmask])
                    tE = em.sop("dve", lambda e, edst=edst, v=v: e.tensor_mul(
                        out=edst, in0=edst,
                        in1=MASKT[:, v * 512:(v + 2) * 512]))
                chain_free[pp] = tE
                ctx["epair"].append(tE)

            def chain_h2(ctx):
                chain_h2a(ctx)
                chain_h2b(ctx)

            for s_i in range(4):
                h = 4 * g + s_i
                rb = 32 * s_i
                ninv = CONSTS[:, 16 + h:17 + h]
                sqk = CONSTS[:, 32 + h:33 + h]
                for Jq in range(NJB):
                    slot = (h * NJB + Jq) % 2
                    njk = 4 * Jq + 4
                    npair = njk // 2
                    e_pair = []
                    for p in range(npair):
                        pp = pipe_pp[0]
                        pipe_pp[0] ^= 1
                        em.wait("pe", [pb_free[2 * pp], pb_free[2 * pp + 1]]
                                + rope_toks)
                        tsc = None
                        for half in range(2):
                            jk = 2 * p + half
                            bview = PB[2 * pp + half]
                            em.op("pe", lambda e, bv=bview, rb=rb, jk=jk, Jq=Jq:
                                  e.matmul(
                                      bv,
                                      R1K[rb:rb + 32, jk * 128:(jk + 1) * 128],
                                      R1Q[rb:rb + 32, Jq * 512:(Jq + 1) * 512],
                                      start=True, stop=False,
                                      tile_position=(rb, 0)))
                            em.op("pe", lambda e, bv=bview, rb=rb, jk=jk, Jq=Jq:
                                  e.matmul(
                                      bv,
                                      R2K[rb:rb + 32, jk * 128:(jk + 1) * 128],
                                      R2Q[rb:rb + 32, Jq * 512:(Jq + 1) * 512],
                                      start=False, stop=False,
                                      tile_position=(rb, 0)))
                            tsc = em.op("pe", lambda e, bv=bview, rb=rb, jk=jk,
                                        Jq=Jq: e.matmul(
                                            bv,
                                            NK0[rb:rb + 1,
                                                jk * 128:(jk + 1) * 128],
                                            Q0B[rb:rb + 1,
                                                Jq * 512:(Jq + 1) * 512],
                                            start=False, stop=True,
                                            tile_position=(rb, 0)),
                                        inc=pe if half == 1 else None)
                        last_tsc[0] = tsc
                        ctx = dict(pp=pp, z=ZP[pp][:, :], jk0=2 * p, Jq=Jq,
                                   slot=slot, tsc=tsc, ninv=ninv, sqk=sqk,
                                   epair=e_pair)
                        if pipe_prev[0] is not None:
                            chain_h2a(pipe_prev[0])
                        chain_h1(ctx)
                        if pipe_prev[0] is not None:
                            chain_h2b(pipe_prev[0])
                        pipe_prev[0] = ctx
                    if pend[0] is not None:
                        flush_attnv(pend[0])
                    pend[0] = (h, Jq, slot, njk, e_pair)
        if pipe_prev[0] is not None:
            chain_h2(pipe_prev[0])
            pipe_prev[0] = None
        if pend[0] is not None:
            flush_attnv(pend[0])
        pb_free[5] = pb5[0]
        pb_free[6] = pb6[0]
        return last_attnv[0], ot_tok[0], ebuf_free

    def emit_outproj(last_attnv, ot_tok, ebuf_free, xa_store_toks):
        wout_toks = []
        xb_store_toks = []
        for m in range(NM):
            ps = m % 2
            em.wait("sp", pool_free[ps])
            tl = dma_in(POOL[ps][:, :], woutT_r[:, m, :])
            em.wait("dve", [tl, last_attnv, ebuf_free[0], ebuf_free[1]])
            tw = em.op("dve", lambda e, ps=ps, m=m: e.tensor_copy(
                out=WOUTB[:, m * 1024:(m + 1) * 1024], in_=POOL[ps][:, :]), inc=dve)
            pool_free[ps] = tw
            wout_toks.append(tw)
        last = None
        xb_free = [None]
        for j in range(NJ):
            xp = 0 if j % 2 == 0 else 2
            em.wait("sp", [pool_free[xp]] + xa_store_toks)
            tl = dma_in(POOL[xp][:, :], xA_d[j * 128:(j + 1) * 128, :])
            em.wait("dve", [pool_free[3], xb_free[0]])
            txb = None
            for G in range(2):
                bank = (j * 2 + G) % 5
                em.wait("pe", [pb_free[bank], ot_tok] + wout_toks)
                tmm = None
                for m in range(NM):
                    tmm = em.op("pe", lambda e, bank=bank, m=m, j=j, G=G: e.matmul(
                        PB[bank],
                        OT[:, m * T + j * 128: m * T + j * 128 + 128],
                        WOUTB[:, m * 1024 + G * 512: m * 1024 + (G + 1) * 512],
                        start=(m == 0), stop=(m == NM - 1)),
                        inc=pe if m == NM - 1 else None)
                em.wait("dve", [tmm, tl])
                txb = em.op("dve", lambda e, bank=bank, G=G, xp=xp: e.tensor_add(
                    out=POOL[1][:, G * 512:(G + 1) * 512], in0=PB[bank],
                    in1=POOL[xp][:, G * 512:(G + 1) * 512]), inc=dve)
                pb_free[bank] = txb
            pool_free[xp] = txb
            em.wait("act", txb)
            tso = dma_out(xB_d[j * 128:(j + 1) * 128, :], POOL[1][:, :])
            xb_store_toks.append(tso)
            toks = [txb, tso]
            if TAP == "xB" or PHASE == 2:
                toks.append(dma_out(y_d[j * 128:(j + 1) * 128, :], POOL[1][:, :]))
            lb = LNBF if j % 2 == 0 else LNBF2
            em.wait("act", lnbf_free[j % 2])
            ta = emit_ln_rows(POOL[1][:, :], POOL[3][:, :], lb[:, :], [txb],
                              nm_store=STNM3[:, j:j + 1], r_store=STR3[:, j:j + 1],
                              alt=(j % 2 == 1))
            pool_free[3] = ta
            last = emit_transpose_chunk(j, ta, src=lb)
            toks.append(ta)
            xb_free[0] = toks
        lnxT_tok[0] = last
        return xb_store_toks

    if PHASE == -1:
        em.wait("act", tok_consts)
        dma_out(y_d[0:128, 0:64], CONSTS[:, :])
    if PHASE >= 0:
        em.scope("p_ln1")
        emit_ln_pass(x_d, STNM1, STR1)
        em.escope()
    em.scope("p_gate1")
    sp1 = (emit_gate() if not LN_ONLY else lnxT_tok[0]) if PHASE >= 0 else None
    em.escope()
    if PHASE == 0:
        em.wait("act", sp1)
        em.wait("act", lnxT_tok[0])
        dma_out(y_d[0:128, 0:128], SPARSE[:, :])
        scr_tok = None
    elif PHASE >= 1:
        em.scope("p_ffn1")
        scr_tok = emit_ffn(True, sp1)
        em.escope()
    else:
        scr_tok = None
    if PHASE >= 1:
        em.scope("p_combA")
        xa_toks = emit_combine(x_d, STNM1, STR1, xA_d,
                               TAP == "xA" or PHASE == 1, True, None, None)
        em.escope()
    if PHASE >= 2:
        em.scope("p_attn")
        la, ot, ebf = emit_attention()
        em.escope()
        em.scope("p_outproj")
        xb_toks = emit_outproj(la, ot, ebf, xa_toks)
        em.escope()
    if PHASE >= 3:
        em.scope("p_gate2")
        sp2 = emit_gate()
        em.escope()
        em.scope("p_ffn2")
        emit_ffn(False, sp2, scr_wait=[scr_tok, lnxT_tok[0]])
        em.escope()
        em.scope("p_combC")
        emit_combine(xB_d, STNM3, STR3, y_d, False, False, None, None,
                     x_wait=xb_toks)
        em.escope()

    for s in ST_SEMS:
        if em.cnt.get(s, 0):
            em.q["act"].append(("w", s, em.cnt[s]))
    for s in LD_SEMS:
        if em.cnt.get(s, 0):
            em.q["sp"].append(("w", s, em.cnt[s]))

    from contextlib import ExitStack
    with ExitStack() as stack:
        for s in LD_SEMS + ST_SEMS + ["pe", "dve", "act", "gp"]:
            em.hw[s] = stack.enter_context(nc.semaphore(s))
        block = stack.enter_context(nc.Block())

        @block.sync
        def _(e):
            em.install(e, "sp", nc)

        @block.tensor
        def _(e):
            em.install(e, "pe", nc)

        @block.vector
        def _(e):
            em.install(e, "dve", nc)

        @block.scalar
        def _(e):
            em.install(e, "act", nc)

        @block.gpsimd
        def _(e):
            em.install(e, "gp", nc)

    return nc


def _host_constants():
    inv_freq = 1.0 / (10000.0 ** (np.arange(0, HD, 2, dtype=np.float64) / HD))
    pos = np.arange(T, dtype=np.float64)
    freqs = pos[:, None] * inv_freq[None, :]
    cos = np.cos(freqs).astype(np.float32)
    sin = np.sin(freqs).astype(np.float32)
    crep = np.empty((128, T), np.float32)
    srep = np.empty((128, T), np.float32)
    for p in range(128):
        crep[p] = cos[:, p % 32]
        srep[p] = sin[:, p % 32]
    maskT = np.zeros((128, 4, 512), np.float32)
    for v in range(4):
        for p in range(128):
            maskT[p, v, p + 128 * v:] = 1.0
    iden = np.eye(128, dtype=np.float32)
    blkd = np.zeros((128, 128), np.float32)
    for p in range(128):
        blkd[p, 32 * (p // 32)] = 1.0
    bf = ml_dtypes.bfloat16
    return (crep.astype(bf), srep.astype(bf), maskT.reshape(128, 2048).astype(bf),
            iden.astype(bf), blkd.astype(bf), np.ones((128, 128), np.float32))


def kernel(L0_attn_x, L0_attn_W_qkv, L0_attn_W_out, L0_attn_hyp_k,
           L0_ffn_gate_w, L0_ffn_expert_bias, L0_ffn_w_up, L0_ffn_w_down):
    x = np.ascontiguousarray(np.asarray(L0_attn_x, np.float32))
    wupT = np.ascontiguousarray(
        np.asarray(L0_ffn_w_up, np.float32).transpose(0, 2, 1))
    wdnT = np.ascontiguousarray(
        np.asarray(L0_ffn_w_down, np.float32).transpose(0, 2, 1))
    wqkvT = np.ascontiguousarray(np.asarray(L0_attn_W_qkv, np.float32).T)
    woutT = np.ascontiguousarray(np.asarray(L0_attn_W_out, np.float32).T)
    gwT = np.ascontiguousarray(np.asarray(L0_ffn_gate_w, np.float32).T)
    eb = np.asarray(L0_ffn_expert_bias, np.float32).reshape(1, E)
    hyp = np.asarray(L0_attn_hyp_k, np.float32).reshape(1, NH)
    kcmap = np.empty((128, 4), np.float32)
    for g in range(4):
        for p in range(128):
            kcmap[p, g] = hyp[0, 4 * g + p // 32]
    crep, srep, maskT, iden, blkd, onesm = _host_constants()

    nc = build_nc(TAP)
    shared = dict(wupT=wupT, wdnT=wdnT, wqkvT=wqkvT, woutT=woutT, gwT=gwT,
                  eb=eb, hyp=hyp, kcmap=kcmap, crep=crep, srep=srep,
                  maskT=maskT, iden=iden, blkd=blkd, onesm=onesm)
    in_maps = [dict(shared, x=np.ascontiguousarray(x[b])) for b in range(8)]
    res = run_bass_kernel_spmd(nc, in_maps, core_ids=list(range(8)))
    return np.stack([r["y"] for r in res.results], axis=0)


if __name__ == "__main__":
    nc = build_nc()
    counts = {e: len([i for i in q if i[0] == "i"]) for e, q in [(None, [])]} \
        if False else None
    print("built ok")
